# version 02
# baseline (speedup 1.0000x reference)
import os
import numpy as np
import ml_dtypes
from contextlib import ExitStack

import concourse.bass as bass
import concourse.mybir as mybir
from concourse.bass_utils import run_bass_kernel_spmd

F32 = mybir.dt.float32
BF16 = mybir.dt.bfloat16
U8 = mybir.dt.uint8
I32 = mybir.dt.int32
AF = mybir.ActivationFunctionType
ALU = mybir.AluOpType
AX = mybir.AxisListType
BF16NP = ml_dtypes.bfloat16

NC = 8
B, S, HID = 2, 4096, 4096
NT = 1024
I1 = 11008
NKT = 32
NIT = 86
EPS = 1e-5
INVSQD = float(1.0 / np.sqrt(np.float32(128.0)))

PI = np.concatenate([np.arange(0, HID, 2), np.arange(1, HID, 2)])
PI_INV = np.empty(HID, np.int64)
PI_INV[PI] = np.arange(HID)
SIG = np.concatenate([np.arange(0, I1, 2), np.arange(1, I1, 2)])

ENGS = ("sync", "gpsimd", "tensor", "vector", "scalar")


def tok_slices(r):
    return [(b, (8 * j + r) * 128, (8 * j + r) * 128 + 128)
            for b in range(B) for j in range(4)]


def gather_tokens(x, r):
    return np.concatenate([x[b, s0:s1] for (b, s0, s1) in tok_slices(r)], axis=0)


def u8_of_u32(a):
    return np.ascontiguousarray(a).view(np.uint8)


def bf16(a):
    return np.ascontiguousarray(a.astype(BF16NP) if a.dtype != BF16NP else a)


class Prog:
    def __init__(self, name="prog"):
        self.nc = bass.Bass("TRN2", target_bir_lowering=False)
        self.ctx = ExitStack()
        self.name = name
        self.sems = {}

    def sem(self, name):
        if name not in self.sems:
            self.sems[name] = self.ctx.enter_context(self.nc.semaphore(name))
        return self.sems[name]

    def sbuf(self, name, shape, dtype):
        return self.ctx.enter_context(self.nc.sbuf_tensor(name, list(shape), dtype))

    def psum(self, name, shape, dtype=F32):
        return self.ctx.enter_context(self.nc.psum_tensor(name, list(shape), dtype))

    def din(self, name, shape, dtype):
        return self.nc.dram_tensor(name, list(shape), dtype, kind="ExternalInput").ap()

    def dout(self, name, shape, dtype):
        return self.nc.dram_tensor(name, list(shape), dtype, kind="ExternalOutput").ap()

    def dint(self, name, shape, dtype, shared=False):
        return self.nc.dram_tensor(name, list(shape), dtype, kind="Internal",
                                   addr_space="Shared" if shared else "Local").ap()

    def block(self):
        return self.ctx.enter_context(self.nc.Block())

    def close(self):
        self.ctx.close()


class Emitter:
    def __init__(self, prog):
        self.p = prog
        self.ops = {e: [] for e in ENGS}
        self.counts = {}

    def sem(self, name):
        self.counts.setdefault(name, 0)
        return self.p.sem(name)

    def val(self, name):
        return self.counts.get(name, 0)

    def emit(self, eng, fn):
        self.ops[eng].append(fn)

    def wait(self, eng, name, val=None):
        s = self.sem(name)
        v = self.counts[name] if val is None else val
        if v > 0:
            self.emit(eng, lambda e, s=s, v=v: e.wait_ge(s, v))
        return v

    def inc(self, name, n):
        self.counts[name] = self.counts.get(name, 0) + n
        return self.counts[name]

    def dma(self, eng, out, in_, sem_name, **kw):
        s = self.sem(sem_name)
        self.inc(sem_name, 16)
        self.emit(eng, lambda e, o=out, i=in_, s=s, kw=kw: e.dma_start(out=o, in_=i, **kw).then_inc(s, 16))

    def sop(self, eng, fn, sem_name=None):
        nm = sem_name or ("ser_" + eng)
        v = self.op(eng, fn, nm)
        self.wait(eng, nm, v)
        return v

    def op(self, eng, fn, sem_name=None):
        if sem_name is None:
            self.emit(eng, fn)
            return None
        s = self.sem(sem_name)
        v = self.inc(sem_name, 1)
        self.emit(eng, lambda e, fn=fn, s=s: fn(e).then_inc(s, 1))
        return v

    def finalize(self, block):
        for ename in ENGS:
            ops = self.ops[ename]
            if not ops:
                continue
            def body(eng, ops=ops):
                for f in ops:
                    f(eng)
            getattr(block, ename)(body)


def build_l1_dequant(p, E, nbgu, nbd):
    WGU = NKT * 128
    WD = NIT * 128
    ins = []
    for nm, nb, W in (("g", nbgu, WGU), ("u", nbgu, WGU), ("d", nbd, WD)):
        q = p.din(f"q{nm}", [nb, 128, W], U8)
        s = p.din(f"s{nm}", [nb, 128, W], F32)
        z = p.din(f"z{nm}", [nb, 128, W], F32)
        o = p.dout(f"w{nm}_o", [nb, 128, W], BF16)
        for i in range(nb):
            ins.append((q, s, z, o, W, i))

    WMAX = max(WGU, WD)
    NCH = 4
    qt = [p.sbuf(f"dq_qt{i}", [128, WMAX], U8) for i in range(2)]
    st_ = p.sbuf("dq_st", [128, WMAX], F32)
    zt = p.sbuf("dq_zt", [128, WMAX], F32)
    ft = p.sbuf("dq_ft", [128, WMAX // NCH], F32)
    qf = p.sbuf("dq_qf", [128, WMAX // NCH], BF16)
    wt = [p.sbuf(f"dq_wt{i}", [128, WMAX], BF16) for i in range(2)]

    vdone = []
    sdone = []

    for n, (q, s, z, o, W, i) in enumerate(ins):
        b = n % 2
        if n >= 2:
            E.wait("sync", "dq_v", vdone[n - 2])
        E.dma("sync", qt[b][:, 0:W], q[i], f"dq_ldq{b}")
        if n >= 1:
            E.wait("sync", "dq_v", vdone[n - 1])
        E.dma("sync", st_[:, 0:W], s[i], "dq_lds")
        E.dma("sync", zt[:, 0:W], z[i], "dq_lds")
        E.wait("vector", f"dq_ldq{b}")
        E.wait("vector", "dq_lds")
        if n >= 2:
            E.wait("vector", "dq_st", sdone[n - 2])
        cw = W // NCH
        for c in range(NCH):
            sl = slice(c * cw, (c + 1) * cw)
            E.op("vector", lambda e, b=b, sl=sl, cw=cw: e.tensor_copy(
                out=qf[:, 0:cw], in_=qt[b][:, sl]), "dq_c1")
            E.wait("vector", "dq_c1")
            E.op("vector", lambda e, b=b, sl=sl, cw=cw: e.tensor_tensor(
                out=ft[:, 0:cw], in0=qf[:, 0:cw], in1=zt[:, sl], op=ALU.subtract),
                "dq_c2")
            E.wait("vector", "dq_c2")
            E.op("vector", lambda e, b=b, sl=sl, cw=cw: e.tensor_tensor(
                out=wt[b][:, sl], in0=ft[:, 0:cw], in1=st_[:, sl], op=ALU.mult),
                "dq_v")
            E.wait("vector", "dq_v")
        vdone.append(E.val("dq_v"))
        E.wait("gpsimd", "dq_v", vdone[n])
        E.dma("gpsimd", o[i], wt[b][:, 0:W], "dq_st")
        sdone.append(E.val("dq_st"))
    E.wait("gpsimd", "dq_st", E.val("dq_st"))
    return p


def prep_l1_dq(d):
    def u8cast(a_int32):
        return np.ascontiguousarray(u8_of_u32(a_int32.astype(np.uint32))[:, ::4])

    qg_T = u8cast(d["L1_ffn_gate_qweight"]).T[PI]
    qu_T = u8cast(d["L1_ffn_up_qweight"]).T[PI]

    def swz_gu(qT):
        a = qT.reshape(NKT, 128, NIT, 128)
        return np.ascontiguousarray(a.transpose(2, 1, 0, 3))

    qg_b = swz_gu(qg_T)
    qu_b = swz_gu(qu_T)
    grp_h = PI // 128

    def swz_scale_gu(sfull):
        a = sfull[:, grp_h]
        a = a.T.reshape(NKT, 128, NIT, 128)
        return np.ascontiguousarray(a.transpose(2, 1, 0, 3).astype(np.float32))

    qd_T = u8cast(d["L1_ffn_down_qweight"])[PI].T

    def swz_d(qT):
        a = qT.reshape(NIT, 128, NKT, 128)
        return np.ascontiguousarray(a.transpose(2, 1, 0, 3))

    qd_b = swz_d(qd_T)

    def swz_scale_d(sfull):
        a = sfull[PI]
        a = a.T.reshape(NIT, 1, NKT, 128)
        a = np.broadcast_to(a, (NIT, 128, NKT, 128))
        return np.ascontiguousarray(a.transpose(2, 1, 0, 3).astype(np.float32))

    sg_b = swz_scale_gu(d["L1_ffn_gate_scales"]); zg_b = swz_scale_gu(d["L1_ffn_gate_zeros"])
    su_b = swz_scale_gu(d["L1_ffn_up_scales"]); zu_b = swz_scale_gu(d["L1_ffn_up_zeros"])
    sd_b = swz_scale_d(d["L1_ffn_down_scales"]); zd_b = swz_scale_d(d["L1_ffn_down_zeros"])

    nbgu, nbd = 11, 4
    in_maps = []
    for r in range(NC):
        gsl = np.minimum(np.arange(r * 11, r * 11 + nbgu), NIT - 1)
        dsl = np.arange(r * 4, r * 4 + nbd)
        c = np.ascontiguousarray
        in_maps.append(dict(
            qg=c(qg_b[gsl]), sg=c(sg_b[gsl]), zg=c(zg_b[gsl]),
            qu=c(qu_b[gsl]), su=c(su_b[gsl]), zu=c(zu_b[gsl]),
            qd=c(qd_b[dsl]), sd=c(sd_b[dsl]), zd=c(zd_b[dsl]),
        ))
    return in_maps, dict(nbgu=nbgu, nbd=nbd)


def assemble_l1_dq(results, meta):
    Wg = np.zeros((NIT, 128, NKT * 128), BF16NP)
    Wu = np.zeros((NIT, 128, NKT * 128), BF16NP)
    Wd = np.zeros((NKT, 128, NIT * 128), BF16NP)
    for r in range(NC):
        gsl = np.minimum(np.arange(r * 11, r * 11 + meta["nbgu"]), NIT - 1)
        for jj, g in enumerate(gsl):
            Wg[g] = results[r]["wg_o"][jj]
            Wu[g] = results[r]["wu_o"][jj]
        for jj, g in enumerate(np.arange(r * 4, r * 4 + meta["nbd"])):
            Wd[g] = results[r]["wd_o"][jj]
    return Wg, Wu, Wd


def build_ffn(p, E, xin, xout, wg, wu, wd, nit, out_dtype, tag, res_scale=1.0,
              stage=3):
    nk = NKT
    x3_sb = p.sbuf(tag + "x3", [128, nk * 512], BF16)
    act_sb = p.sbuf(tag + "act", [128, nit * 512], BF16)
    wgb = [p.sbuf(tag + f"wgb{i}", [128, nk * 128], BF16) for i in range(2)]
    wub = [p.sbuf(tag + f"wub{i}", [128, nk * 128], BF16) for i in range(2)]
    wdb = [p.sbuf(tag + "wdb0", [128, nit * 128], BF16)] * 2
    sg_sb = [p.sbuf(tag + f"sg{i}", [128, 512], F32) for i in range(2)]
    sq_sb = [p.sbuf(tag + f"sq{i}", [128, 512], BF16) for i in range(2)]
    ones_bf = p.sbuf(tag + "ones", [128, 1], BF16)
    ones1 = p.sbuf(tag + "ones1", [1, 128], F32)
    stv = p.sbuf(tag + "stv", [1, 512 * 6], F32)
    mqn = stv[0:1, 2560:3072]
    ab_sb = p.sbuf(tag + "ab", [128, 512], F32)
    bb_sb = p.sbuf(tag + "bb", [128, 512], F32)
    out_sc = [p.sbuf(tag + f"osc{i}", [128, 512], out_dtype) for i in range(2)]
    t1_sb = p.sbuf(tag + "t1", [128, 512], F32)

    pg = [p.psum(tag + f"pg{i}", [128, 512]) for i in range(2)]
    pu = [p.psum(tag + f"pu{i}", [128, 512]) for i in range(2)]
    pd = [p.psum(tag + f"pd{i}", [128, 512]) for i in range(2)]
    pstat1 = p.psum(tag + "pstat1", [1, 512])
    pstat2 = p.psum(tag + "pstat2", [1, 512])
    pbcA = pg[0]
    pbcB = pu[0]

    onesb_in = p.din(tag + "onesb_in", [128, 1], BF16)
    ones1_in = p.din(tag + "ones1_in", [1, 128], F32)
    E.dma("sync", ones_bf[:], onesb_in, tag + "c")
    E.dma("sync", ones1[:], ones1_in, tag + "c")
    cval = E.val(tag + "c")

    MSL = lambda a, i: a[:, i * 128:(i + 1) * 128]
    CSL = lambda a, i: a[:, i * 512:(i + 1) * 512]

    for tc in range(2):
        if tc > 0:
            E.wait("sync", tag + "mm", E.val(tag + "mm"))
            E.wait("sync", tag + "ln", E.val(tag + "ln"))
            E.wait("sync", tag + "f4", E.val(tag + "f4"))
            E.wait("sync", tag + "sq", E.val(tag + "sq"))
            E.wait("sync", tag + "act", E.val(tag + "act"))
            for bb2 in range(2):
                E.wait("sync", tag + f"out{bb2}", E.val(tag + f"out{bb2}"))
                E.wait("vector", tag + f"out{bb2}", E.val(tag + f"out{bb2}"))
        for k in range(nk):
            E.dma("sync", CSL(x3_sb, k),
                  xin[k * 128:(k + 1) * 128, tc * 512:(tc + 1) * 512], tag + "x")
        xval = E.val(tag + "x")

        E.wait("tensor", tag + "x", xval)
        E.wait("tensor", tag + "c", cval)
        if tc > 0:
            E.wait("tensor", tag + "f5", E.val(tag + "f5"))
        for it in range(nit):
            bb = it % 2
            n = tc * nit + it
            if n >= 2:
                E.wait("sync", tag + "mm", 2 * (n - 1))
            E.dma("sync", wgb[bb][:], wg[it], tag + f"w{bb}")
            E.dma("sync", wub[bb][:], wu[it], tag + f"w{bb}")
            E.wait("tensor", tag + f"w{bb}")
            if n >= 2:
                E.wait("tensor", tag + "act", n - 1)
                E.wait("tensor", tag + "sil", n - 1)
            def mmg(e, bb=bb):
                for k in range(nk - 1):
                    e.matmul(pg[bb][:], lhsT=MSL(wgb[bb], k), rhs=CSL(x3_sb, k),
                             start=(k == 0), stop=False)
                return e.matmul(pg[bb][:], lhsT=MSL(wgb[bb], nk - 1),
                                rhs=CSL(x3_sb, nk - 1), start=False, stop=True)
            E.op("tensor", mmg, tag + "mm")
            E.wait("scalar", tag + "mm", E.val(tag + "mm"))
            if n >= 2:
                E.wait("scalar", tag + "act", n - 1)
            E.op("scalar", lambda e, bb=bb: e.activation(
                out=sg_sb[bb][:], in_=pg[bb][:], func=AF.Silu), tag + "sil")
            def mmu(e, bb=bb):
                for k in range(nk - 1):
                    e.matmul(pu[bb][:], lhsT=MSL(wub[bb], k), rhs=CSL(x3_sb, k),
                             start=(k == 0), stop=False)
                return e.matmul(pu[bb][:], lhsT=MSL(wub[bb], nk - 1),
                                rhs=CSL(x3_sb, nk - 1), start=False, stop=True)
            E.op("tensor", mmu, tag + "mm")
            E.wait("vector", tag + "sil", E.val(tag + "sil"))
            E.wait("vector", tag + "mm", E.val(tag + "mm"))
            E.op("vector", lambda e, bb=bb, it=it: e.tensor_tensor(
                out=CSL(act_sb, it), in0=pu[bb][:], in1=sg_sb[bb][:],
                op=ALU.mult), tag + "act")
        act_all = E.val(tag + "act")
        if stage == 1:
            for it in range(nit):
                E.wait("gpsimd", tag + "act", act_all)
                E.dma("gpsimd", xout[it * 128:(it + 1) * 128, tc * 512:(tc + 1) * 512],
                      CSL(act_sb, it), tag + "out0")
            continue

        for k in range(nk):
            bb = k % 2
            n = tc * nk + k
            if n >= 1:
                E.wait("gpsimd", tag + "dmm", n)
            E.dma("gpsimd", wdb[bb][:], wd[k], tag + "wd")
            E.wait("tensor", tag + "wd", E.val(tag + "wd"))
            E.wait("tensor", tag + "act", act_all)
            if n >= 2:
                E.wait("tensor", tag + "ep", n - 1)
            def mmd(e, bb=bb):
                for it in range(nit - 1):
                    e.matmul(pd[bb][:], lhsT=MSL(wdb[bb], it), rhs=CSL(act_sb, it),
                             start=(it == 0), stop=False)
                return e.matmul(pd[bb][:], lhsT=MSL(wdb[bb], nit - 1),
                                rhs=CSL(act_sb, nit - 1), start=(nit == 1), stop=True)
            E.op("tensor", mmd, tag + "dmm")
            E.wait("vector", tag + "dmm", E.val(tag + "dmm"))
            E.wait("vector", tag + "mm", E.val(tag + "mm"))
            if res_scale == 1.0:
                E.sop("vector", lambda e, bb=bb, k=k: e.tensor_tensor(
                    out=CSL(x3_sb, k), in0=pd[bb][:], in1=CSL(x3_sb, k),
                    op=ALU.add), tag + "ep")
            else:
                E.sop("vector", lambda e, bb=bb, k=k: e.scalar_tensor_tensor(
                    out=CSL(x3_sb, k), in0=CSL(x3_sb, k), scalar=float(res_scale),
                    in1=pd[bb][:], op0=ALU.mult, op1=ALU.add), tag + "ep")
            E.wait("scalar", tag + "ep", E.val(tag + "ep"))
            if n >= 2:
                E.wait("scalar", tag + "stat", (n - 1))
            E.op("scalar", lambda e, bb=bb, k=k: e.activation(
                out=sq_sb[bb][:], in_=CSL(x3_sb, k), func=AF.Square), tag + "sq")
            E.wait("tensor", tag + "sq", E.val(tag + "sq"))
            if tc == 1 and k == 0:
                E.wait("tensor", tag + "f1", E.val(tag + "f1"))
            def mms(e, bb=bb, k=k):
                e.matmul(pstat1[:], lhsT=ones_bf[:], rhs=CSL(x3_sb, k),
                         start=(k == 0), stop=(k == nk - 1))
                return e.matmul(pstat2[:], lhsT=ones_bf[:], rhs=sq_sb[bb][:],
                                start=(k == 0), stop=(k == nk - 1))
            E.op("tensor", mms, tag + "stat")
        if stage == 2:
            E.wait("gpsimd", tag + "ep", E.val(tag + "ep"))
            for k in range(nk):
                E.dma("gpsimd", xout[k * 128:(k + 1) * 128, tc * 512:(tc + 1) * 512],
                      CSL(x3_sb, k), tag + "out0")
            continue

        E.wait("vector", tag + "stat", E.val(tag + "stat"))
        SL = lambda i: stv[0:1, i * 512:(i + 1) * 512]
        E.sop("vector", lambda e: e.tensor_scalar(out=SL(0), in0=pstat1[:],
              scalar1=1.0 / HID, scalar2=None, op0=ALU.mult))
        E.sop("vector", lambda e: e.tensor_scalar(out=SL(1), in0=pstat2[:],
              scalar1=1.0 / HID, scalar2=None, op0=ALU.mult))
        E.sop("vector", lambda e: e.tensor_tensor(out=SL(3), in0=SL(0), in1=SL(0),
              op=ALU.mult))
        E.sop("vector", lambda e: e.tensor_tensor(out=SL(2), in0=SL(1), in1=SL(3),
              op=ALU.subtract))
        E.sop("vector", lambda e: e.tensor_scalar(out=SL(3), in0=SL(2),
              scalar1=EPS, scalar2=None, op0=ALU.add), tag + "f1")
        E.wait("scalar", tag + "f1", E.val(tag + "f1"))
        E.op("scalar", lambda e: e.activation(out=SL(2), in_=SL(3), func=AF.Sqrt),
             tag + "f2")
        E.wait("vector", tag + "f2", E.val(tag + "f2"))
        E.sop("vector", lambda e: e.reciprocal(out=SL(4), in_=SL(2)))
        E.sop("vector", lambda e: e.tensor_tensor(out=SL(3), in0=SL(0), in1=SL(4),
              op=ALU.mult))
        E.sop("vector", lambda e: e.tensor_scalar(out=SL(5), in0=SL(3),
              scalar1=-1.0, scalar2=None, op0=ALU.mult), tag + "f3")
        E.wait("tensor", tag + "f3", E.val(tag + "f3"))
        E.wait("tensor", tag + "sil", E.val(tag + "sil"))
        E.op("tensor", lambda e: e.matmul(pbcA[:], lhsT=ones1[:], rhs=SL(4),
             start=True, stop=True))
        E.op("tensor", lambda e: e.matmul(pbcB[:], lhsT=ones1[:], rhs=SL(5),
             start=True, stop=True), tag + "f4")
        E.wait("vector", tag + "f4", E.val(tag + "f4"))
        E.sop("vector", lambda e: e.tensor_copy(out=ab_sb[:], in_=pbcA[:]))
        E.sop("vector", lambda e: e.tensor_copy(out=bb_sb[:], in_=pbcB[:]), tag + "f5")

        for k in range(nk):
            bb = k % 2
            n = tc * nk + k
            if n >= 2:
                E.wait("vector", tag + f"out{bb}", 16 * ((n - 2) // 2 + 1))
            E.sop("vector", lambda e, k=k: e.tensor_tensor(
                out=t1_sb[:], in0=CSL(x3_sb, k), in1=ab_sb[:], op=ALU.mult))
            E.sop("vector", lambda e, bb=bb: e.tensor_tensor(
                out=out_sc[bb][:], in0=t1_sb[:], in1=bb_sb[:], op=ALU.add),
                tag + "ln")
            E.wait("gpsimd", tag + "ln", E.val(tag + "ln"))
            E.dma("gpsimd", xout[k * 128:(k + 1) * 128, tc * 512:(tc + 1) * 512],
                  out_sc[bb][:], tag + f"out{bb}")
    for bb2 in range(2):
        E.wait("gpsimd", tag + f"out{bb2}", E.val(tag + f"out{bb2}"))
        E.wait("sync", tag + f"out{bb2}", E.val(tag + f"out{bb2}"))
    E.wait("sync", tag + "out0", E.val(tag + "out0"))
    return p


def build_l0_dequant(p, E, ntgu, ntd):
    jobs = []
    for nm, nt, W, nslab in (("gu", ntgu, 22016, 2), ("dn", ntd, 4096, 1)):
        q = p.din(f"l0q{nm}", [nt, 128, W], U8)
        sr = p.din(f"l0s{nm}", [nt, 128, W], F32)
        zb = p.din(f"l0z{nm}", [nt, 128, W // 2], U8)
        lo_o = p.dout(f"l0lo{nm}", [nt, 128, W], BF16)
        hi_o = p.dout(f"l0hi{nm}", [nt, 128, W], BF16)
        ws = W // nslab
        for i in range(nt):
            for sl_i in range(nslab):
                jobs.append((q, sr, zb, lo_o, hi_o, W, i, sl_i * ws, ws))

    WM = 11008
    qt = [p.sbuf(f"l0_qt{i}", [128, WM], U8) for i in range(2)]
    srt = p.sbuf("l0_sr", [128, WM], F32)
    zbt = p.sbuf("l0_zb", [128, WM // 2], U8)
    zft = p.sbuf("l0_zf", [128, WM], F32)
    nib = p.sbuf("l0_nib", [128, WM], U8)
    nbf = p.sbuf("l0_nbf", [128, WM // 4], BF16)
    ft = p.sbuf("l0_ft", [128, WM // 4], F32)
    wt = [p.sbuf(f"l0_wt{i}", [128, WM], BF16) for i in range(2)]

    NCH = 4
    vdone = []
    sdone = []
    for n, (q, sr, zb, lo_o, hi_o, Wfull, i, c0, W) in enumerate(jobs):
        b = n % 2
        if n >= 2:
            E.wait("sync", "l0_v", vdone[n - 2])
        E.dma("sync", qt[b][:, 0:W], q[i][:, c0:c0 + W], f"l0_ldq{b}")
        if n >= 1:
            E.wait("sync", "l0_v", vdone[n - 1])
        E.dma("sync", srt[:, 0:W], sr[i][:, c0:c0 + W], "l0_lds")
        E.dma("sync", zbt[:, 0:W // 2], zb[i][:, c0 // 2:(c0 + W) // 2], "l0_lds")
        E.wait("vector", f"l0_ldq{b}")
        E.wait("vector", "l0_lds")
        if n >= 2:
            E.wait("vector", "l0_st", sdone[n - 2])
        E.sop("vector", lambda e, W=W: e.tensor_single_scalar(
            out=nib[:, 0:W // 2], in_=zbt[:, 0:W // 2], scalar=15, op=ALU.bitwise_and))
        E.sop("vector", lambda e, W=W: e.tensor_scalar(
            out=zft[:, 0:W:2], in0=nib[:, 0:W // 2], scalar1=1.0, scalar2=None,
            op0=ALU.add))
        E.sop("vector", lambda e, W=W: e.tensor_single_scalar(
            out=nib[:, 0:W // 2], in_=zbt[:, 0:W // 2], scalar=4,
            op=ALU.logical_shift_right))
        E.sop("vector", lambda e, W=W: e.tensor_scalar(
            out=zft[:, 1:W:2], in0=nib[:, 0:W // 2], scalar1=1.0, scalar2=None,
            op0=ALU.add))
        E.sop("vector", lambda e, W=W: e.tensor_tensor(
            out=zft[:, 0:W], in0=zft[:, 0:W], in1=srt[:, 0:W], op=ALU.mult))
        for hi in range(2):
            if hi == 0:
                E.sop("vector", lambda e, b=b, W=W: e.tensor_single_scalar(
                    out=nib[:, 0:W], in_=qt[b][:, 0:W], scalar=15, op=ALU.bitwise_and))
            else:
                E.wait("vector", "l0_st", E.val("l0_st"))
                E.sop("vector", lambda e, b=b, W=W: e.tensor_single_scalar(
                    out=nib[:, 0:W], in_=qt[b][:, 0:W], scalar=4,
                    op=ALU.logical_shift_right))
            cw = W // NCH
            for c in range(NCH):
                sl = slice(c * cw, (c + 1) * cw)
                E.sop("vector", lambda e, sl=sl, cw=cw: e.tensor_copy(
                    out=nbf[:, 0:cw], in_=nib[:, sl]))
                E.sop("vector", lambda e, sl=sl, cw=cw: e.tensor_tensor(
                    out=ft[:, 0:cw], in0=nbf[:, 0:cw], in1=srt[:, sl], op=ALU.mult))
                E.sop("vector", lambda e, b=b, sl=sl, cw=cw: e.tensor_tensor(
                    out=wt[b][:, sl], in0=ft[:, 0:cw], in1=zft[:, sl],
                    op=ALU.subtract), "l0_v")
            E.wait("gpsimd", "l0_v", E.val("l0_v"))
            E.dma("gpsimd", (hi_o if hi else lo_o)[i][:, c0:c0 + W],
                  wt[b][:, 0:W], "l0_st")
        vdone.append(E.val("l0_v"))
        sdone.append(E.val("l0_st"))
    E.wait("gpsimd", "l0_st", E.val("l0_st"))
    return p


def prep_l0_dq(d):
    qb = u8_of_u32(d["L0_ffn_qweight_gu"].astype(np.uint32)).reshape(512, 22016, 4)
    qb = np.ascontiguousarray(qb.transpose(0, 2, 1).reshape(2048, 22016))
    zb = u8_of_u32(d["L0_ffn_qzeros_gu"].astype(np.uint32)).reshape(32, 2752, 4)
    zb = np.ascontiguousarray(zb.reshape(32, 11008))
    sc = d["L0_ffn_scales_gu"].astype(np.float32)
    grp = (np.arange(2048) // 64)
    qbd = u8_of_u32(d["L0_ffn_qweight_down"].astype(np.uint32)).reshape(1376, 4096, 4)
    qbd = np.ascontiguousarray(qbd.transpose(0, 2, 1).reshape(5504, 4096))
    zbd = u8_of_u32(d["L0_ffn_qzeros_down"].astype(np.uint32)).reshape(86, 512, 4)
    zbd = np.ascontiguousarray(zbd.reshape(86, 2048))
    scd = d["L0_ffn_scales_down"].astype(np.float32)
    grpd = (np.arange(5504) // 64)

    ntgu, ntd = 2, 6
    in_maps = []
    for r in range(NC):
        gurows = np.arange(r * 256, r * 256 + 256)
        drows = np.minimum(np.arange(r * 768, r * 768 + 768), 5503)
        c = np.ascontiguousarray
        in_maps.append(dict(
            l0qgu=c(qb[gurows].reshape(2, 128, 22016)),
            l0sgu=c(sc[grp[gurows]].reshape(2, 128, 22016)),
            l0zgu=c(zb[grp[gurows]].reshape(2, 128, 11008)),
            l0qdn=c(qbd[drows].reshape(6, 128, 4096)),
            l0sdn=c(scd[grpd[drows]].reshape(6, 128, 4096)),
            l0zdn=c(zbd[grpd[drows]].reshape(6, 128, 2048)),
        ))
    return in_maps, dict(ntgu=ntgu, ntd=ntd)


def assemble_l0_dq(results, meta):
    Wgu = np.zeros((4096, 22016), BF16NP)
    Wdn = np.zeros((11008, 4096), BF16NP)
    for r in range(NC):
        lo = results[r]["l0logu"]; hi = results[r]["l0higu"]
        for t in range(2):
            rows = np.arange(r * 256 + t * 128, r * 256 + (t + 1) * 128)
            Wgu[rows] = lo[t]
            Wgu[2048 + rows] = hi[t]
        lod = results[r]["l0lodn"]; hid = results[r]["l0hidn"]
        for t in range(6):
            rows = np.minimum(np.arange(r * 768 + t * 128, r * 768 + (t + 1) * 128), 5503)
            Wdn[rows] = lod[t]
            Wdn[5504 + rows] = hid[t]
    return Wgu, Wdn


def l0_streams(Wgu, Wdn):
    W = Wgu.reshape(32, 128, 22016)
    wg = np.zeros((86, 128, 32, 128), BF16NP)
    wu = np.zeros((86, 128, 32, 128), BF16NP)
    for j in range(86):
        if j < 43:
            cols = np.arange(256 * j, 256 * j + 256, 2)
        else:
            cols = np.arange(256 * (j - 43) + 1, 256 * (j - 43) + 256, 2)
        wg[j] = W[:, :, cols].transpose(1, 0, 2)
        wu[j] = W[:, :, 11008 + cols].transpose(1, 0, 2)
    wg = np.ascontiguousarray(wg.reshape(86, 128, 4096))
    wu = np.ascontiguousarray(wu.reshape(86, 128, 4096))
    Wd = Wdn[:, PI]
    wd = Wd.reshape(86, 128, 32, 128).transpose(2, 1, 0, 3)
    wd = np.ascontiguousarray(wd.reshape(32, 128, 86 * 128))
    return wg, wu, wd


def build_l1_attn(p, E, xin, xout, out_dtype, tag="ha"):
    nk = NKT
    wdn_in = p.din(tag + "wdn", [128, 4096], F32)
    bdn_in = p.din(tag + "bdn", [128, 1], F32)
    wup_in = p.din(tag + "wup", [128, 4096], F32)
    bup_in = p.din(tag + "bup", [128, 32], F32)
    inw_in = p.din(tag + "inw", [128, 768], F32)
    cvec_in = p.din(tag + "cvec", [128, 24], F32)
    dtb_in = p.din(tag + "dtb", [36, 1], F32)
    dvec_in = p.din(tag + "dvec", [4, 1], F32)
    fcd_in = p.din(tag + "fcd", [128, 8], F32)
    nw_in = p.din(tag + "nw", [128, 2], F32)
    outw_in = p.din(tag + "outw", [128, 256], F32)
    gate_in = p.din(tag + "gate", [1, 1], F32)
    ones16_in = p.din(tag + "o16", [16, 1], F32)
    ones1_in = p.din(tag + "o1", [1, 128], F32)
    exp4_in = p.din(tag + "e4", [4, 256], F32)
    onesb_in = p.din(tag + "onesb", [128, 1], BF16)

    x2_sb = p.sbuf(tag + "x2", [128, nk * 1024], BF16)
    wdnb = p.sbuf(tag + "wdnb", [128, 4096], BF16)
    wupb = p.sbuf(tag + "wupb", [128, 4096], BF16)
    inwb = p.sbuf(tag + "inwb", [128, 768], BF16)
    cvec = p.sbuf(tag + "cvecs", [128, 24], F32)
    bdn = p.sbuf(tag + "bdns", [128, 1], F32)
    bup = p.sbuf(tag + "bups", [128, 32], F32)
    dtb = p.sbuf(tag + "dtbs", [36, 1], F32)
    dvec = p.sbuf(tag + "dvecs", [4, 1], F32)
    fcdb = p.sbuf(tag + "fcdbs", [128, 8], BF16)
    nw = p.sbuf(tag + "nws", [128, 2], F32)
    outwb = p.sbuf(tag + "outwbs", [128, 256], BF16)
    o16b = p.sbuf(tag + "o16b", [16, 1], BF16)
    o1 = p.sbuf(tag + "o1s", [1, 128], F32)
    e4b = p.sbuf(tag + "e4b", [4, 256], BF16)
    gt = p.sbuf(tag + "gts", [1, 1], F32)
    ones_bf = p.sbuf(tag + "onesbs", [128, 1], BF16)
    sm4 = p.sbuf(tag + "sm4", [128, 16], F32)
    u0 = p.sbuf(tag + "u0", [128, 1024], F32)
    ub = p.sbuf(tag + "ub", [128, 1024], BF16)
    un = p.sbuf(tag + "un", [128, 1024], F32)
    zx = p.sbuf(tag + "zx", [128, 6 * 1024], F32)
    xsc = p.sbuf(tag + "xsc", [128, 3 * 1024], F32)
    xsb = p.sbuf(tag + "xsb", [128, 2 * 1024], BF16)
    prod = p.sbuf(tag + "prod", [16, 1024], BF16)
    cmt = p.sbuf(tag + "cmt", [16, 1024], F32)
    gv = p.sbuf(tag + "gv", [1, 1024], F32)
    sp4 = p.sbuf(tag + "sp4", [4, 1024], F32)
    s4b = p.sbuf(tag + "s4b", [4, 1024], BF16)
    sps = p.sbuf(tag + "sps", [4, 1024], F32)
    r4 = p.sbuf(tag + "r4", [4, 1024], F32)
    r4b = p.sbuf(tag + "r4b", [4, 1024], BF16)
    yt = p.sbuf(tag + "yt", [128, 2 * 1024], F32)
    sqz = p.sbuf(tag + "sqz", [128, 2048], BF16)
    ynb = p.sbuf(tag + "ynb", [128, 2 * 1024], BF16)
    msr = p.sbuf(tag + "msr", [1, 1024], F32)
    alv = p.sbuf(tag + "alv", [128, 4], F32)
    puc = p.sbuf(tag + "puc", [128, 1024], BF16)
    scr = p.sbuf(tag + "scr", [128, 1024], F32)
    sq_sb = p.sbuf(tag + "sq", [128, 512], BF16)
    stv = p.sbuf(tag + "stv", [1, 512 * 6], F32)
    mqn = stv[0:1, 2560:3072]
    ab_sb = p.sbuf(tag + "ab", [128, 512], F32)
    bb_sb = p.sbuf(tag + "bb", [128, 512], F32)
    out_sc = [p.sbuf(tag + f"osc{i}", [128, 512], out_dtype) for i in range(2)]
    t1_sb = p.sbuf(tag + "t1", [128, 512], F32)

    pA = p.psum(tag + "pA", [128, 512])
    pB = p.psum(tag + "pB", [128, 512])
    pC = p.psum(tag + "pC", [128, 512])
    pD2 = p.psum(tag + "pD", [128, 512])
    pS1 = p.psum(tag + "pS1", [1, 512])
    pS2 = p.psum(tag + "pS2", [1, 512])
    pG = p.psum(tag + "pG", [4, 1024])

    CH = lambda a, c: a[:, c * 512:(c + 1) * 512]
    TL = lambda a, t: a[:, t * 1024:(t + 1) * 1024]
    TC = lambda a, t, c: a[:, t * 1024 + c * 512:t * 1024 + (c + 1) * 512]
    KC = lambda a, k, c: a[:, k * 1024 + c * 512:k * 1024 + (c + 1) * 512]

    def t_wait_v(n=None):
        E.wait("tensor", tag + "vev", E.val(tag + "vev") if n is None else n)

    def v_wait_t():
        E.wait("vector", tag + "mmv", E.val(tag + "mmv"))

    for dst, src in ((cvec, cvec_in), (bdn, bdn_in), (bup, bup_in),
                     (dtb, dtb_in), (dvec, dvec_in), (nw, nw_in),
                     (o1, ones1_in), (gt, gate_in), (ones_bf, onesb_in)):
        E.dma("sync", dst[:], src, tag + "c")
    for dst, src in ((wdnb, wdn_in), (wupb, wup_in), (inwb, inw_in),
                     (outwb, outw_in), (fcdb, fcd_in), (o16b, ones16_in),
                     (e4b, exp4_in)):
        E.dma("gpsimd", dst[:], src, tag + "cg")
    E.wait("vector", tag + "c", E.val(tag + "c"))
    E.wait("vector", tag + "cg", E.val(tag + "cg"))
    E.wait("tensor", tag + "cg", E.val(tag + "cg"))
    E.wait("scalar", tag + "cg", E.val(tag + "cg"))
    E.op("vector", lambda e: e.memset(alv[:, 3:4], 0.0), tag + "cc")
    for k in range(nk):
        E.dma("sync", a := x2_sb[:, k * 1024:(k + 1) * 1024],
              xin[k * 128:(k + 1) * 128, :], tag + "x")
    xval = E.val(tag + "x")

    E.wait("scalar", tag + "c", E.val(tag + "c"))
    E.op("scalar", lambda e: e.activation(out=stv[0:1, 0:1], in_=gt[:],
         func=AF.Sigmoid), tag + "sg")
    E.wait("tensor", tag + "sg", E.val(tag + "sg"))
    E.wait("tensor", tag + "cc", E.val(tag + "cc"))
    E.op("tensor", lambda e: e.matmul(pA[:, 0:1], lhsT=o1[:], rhs=stv[0:1, 0:1],
         start=True, stop=True), tag + "mmv")
    v_wait_t()
    E.sop("vector", lambda e: e.tensor_copy(out=alv[:, 0:1], in_=pA[:, 0:1]))
    E.sop("vector", lambda e: e.tensor_scalar(out=alv[:, 1:2], in0=alv[:, 0:1],
          scalar1=-1.0, scalar2=1.0, op0=ALU.mult, op1=ALU.add))
    E.sop("vector", lambda e: e.tensor_scalar(out=alv[:, 2:3], in0=alv[:, 0:1],
          scalar1=1.0, scalar2=None, op0=ALU.add), tag + "vev")

    E.wait("tensor", tag + "x", xval)
    for c in range(2):
        ps = pA if c == 0 else pB
        t_wait_v()
        def mdn(e, c=c, ps=ps):
            for k in range(nk - 1):
                e.matmul(ps[:], lhsT=wdnb[:, k * 128:(k + 1) * 128],
                         rhs=KC(x2_sb, k, c), start=(k == 0), stop=False)
            return e.matmul(ps[:], lhsT=wdnb[:, (nk - 1) * 128:nk * 128],
                            rhs=KC(x2_sb, nk - 1, c), start=False, stop=True)
        E.op("tensor", mdn, tag + "mmv")
        E.wait("scalar", tag + "mmv", E.val(tag + "mmv"))
        E.op("scalar", lambda e, c=c, ps=ps: e.activation(
            out=CH(u0, c), in_=ps[:], func=AF.Gelu, bias=bdn[:]), tag + "gel")
    E.wait("vector", tag + "gel", E.val(tag + "gel"))
    E.sop("vector", lambda e: e.tensor_copy(out=ub[:], in_=u0[:]), tag + "vev")

    for call in range(2):
        for t in range(6):
            for c in range(2):
                ps = pC if (2 * t + c) % 2 == 0 else pD2
                t_wait_v()
                E.op("tensor", lambda e, t=t, c=c, ps=ps: e.matmul(
                    ps[:], lhsT=inwb[:, t * 128:(t + 1) * 128], rhs=CH(ub, c),
                    start=True, stop=True), tag + "mmv")
                v_wait_t()
                E.sop("vector", lambda e, t=t, c=c, ps=ps: e.tensor_copy(
                    out=TC(zx, t, c), in_=ps[:]), tag + "vev")
        for tt in range(3):
            t = tt + 2
            w5 = cvec[:, t * 3:t * 3 + 1]
            w6 = cvec[:, t * 3 + 1:t * 3 + 2]
            cb = cvec[:, t * 3 + 2:t * 3 + 3]
            E.sop("vector", lambda e, t=t, tt=tt, w6=w6, cb=cb: e.tensor_scalar(
                out=CH(scr, 0), in0=TC(zx, t, 0), scalar1=w6, scalar2=cb,
                op0=ALU.mult, op1=ALU.add))
            E.sop("vector", lambda e, t=t, tt=tt, w5=w5, cb=cb: e.tensor_scalar(
                out=CH(scr, 1), in0=TC(zx, t, 0), scalar1=w5, scalar2=cb,
                op0=ALU.mult, op1=ALU.add))
            E.sop("vector", lambda e, t=t, tt=tt, w6=w6: e.scalar_tensor_tensor(
                out=CH(scr, 1), in0=TC(zx, t, 1), scalar=w6, in1=CH(scr, 1),
                op0=ALU.mult, op1=ALU.add), tag + "cnv")
            E.wait("scalar", tag + "cnv", E.val(tag + "cnv"))
            E.op("scalar", lambda e, tt=tt: e.activation(
                out=TL(xsc, tt), in_=scr[:], func=AF.Silu), tag + "sil")
            if tt < 2:
                E.wait("vector", tag + "sil", E.val(tag + "sil"))
                E.sop("vector", lambda e, tt=tt: e.tensor_copy(
                    out=TL(xsb, tt), in_=TL(xsc, tt)), tag + "vev")
        E.wait("vector", tag + "sil", E.val(tag + "sil"))
        E.sop("vector", lambda e: e.tensor_copy(
            out=cmt[:], in_=xsc[32:48, 2 * 1024:3 * 1024]))
        E.sop("vector", lambda e: e.tensor_tensor(
            out=prod[:], in0=xsc[0:16, 2 * 1024:3 * 1024],
            in1=cmt[:], op=ALU.mult), tag + "vev")
        for c in range(2):
            t_wait_v()
            E.op("tensor", lambda e, c=c: e.matmul(
                pS1[:], lhsT=o16b[:], rhs=CH(prod, c), start=True, stop=True),
                tag + "mmv")
            v_wait_t()
            E.sop("vector", lambda e, c=c: e.tensor_copy(out=CH(gv, c), in_=pS1[:]),
                  tag + "vev")
        for outb in range(2):
            srcrow = (32, 36) if outb == 0 else (0, 4)
            sc_ = 1 - outb
            src = zx[srcrow[0]:srcrow[1], 5 * 1024 + sc_ * 512:5 * 1024 + (sc_ + 1) * 512]
            dslice = dtb[srcrow[0]:srcrow[1], :]
            E.sop("vector", lambda e, src=src, outb=outb, dslice=dslice: e.tensor_scalar(
                out=sps[0:4, outb * 512:(outb + 1) * 512], in0=src, scalar1=dslice,
                scalar2=None, op0=ALU.add), tag + "sp1")
        E.wait("scalar", tag + "sp1", E.val(tag + "sp1"))
        E.op("scalar", lambda e: e.activation(out=sps[0:4, 1024:2048] if False else sp4[0:4, :],
             in_=sps[0:4, 0:1024], func=AF.Abs), tag + "sp2")
        E.wait("scalar", tag + "sp2", E.val(tag + "sp2"))
        E.op("scalar", lambda e: e.activation(out=sp4[0:4, :], in_=sp4[0:4, :],
             func=AF.Exp, scale=-1.0), tag + "sp3")
        E.wait("scalar", tag + "sp3", E.val(tag + "sp3"))
        E.op("scalar", lambda e: e.activation(out=sp4[0:4, :], in_=sp4[0:4, :],
             func=AF.Ln, bias=1.0), tag + "sp4")
        E.wait("vector", tag + "sp4", E.val(tag + "sp4"))
        E.sop("vector", lambda e: e.tensor_relu(out=sps[0:4, 0:1024], in_=sps[0:4, 0:1024]))
        E.sop("vector", lambda e: e.tensor_tensor(out=sp4[0:4, :], in0=sp4[0:4, :],
              in1=sps[0:4, 0:1024], op=ALU.add), tag + "vev")
        for c in range(2):
            t_wait_v()
            E.op("tensor", lambda e, c=c: e.matmul(
                pG[:, c * 512:(c + 1) * 512], lhsT=o1[0:1, 0:4], rhs=CH(gv, c),
                start=True, stop=True), tag + "mmv")
        v_wait_t()
        for c in range(2):
            E.sop("vector", lambda e, c=c: e.tensor_tensor(
                out=CH(sp4, c), in0=CH(sp4, c),
                in1=pG[0:4, (1 - c) * 512:(2 - c) * 512], op=ALU.mult))
        E.sop("vector", lambda e: e.tensor_copy(out=s4b[0:4, :], in_=sp4[0:4, :]),
              tag + "vev")
        for tt in range(2):
            for c in range(2):
                ps = pC if c == 0 else pD2
                t_wait_v()
                E.op("tensor", lambda e, tt=tt, c=c, ps=ps: e.matmul(
                    ps[:], lhsT=e4b[:, tt * 128:(tt + 1) * 128], rhs=CH(s4b, c),
                    start=True, stop=True), tag + "mmv")
                v_wait_t()
                E.sop("vector", lambda e, tt=tt, c=c, ps=ps: e.tensor_tensor(
                    out=TC(yt, tt, c), in0=ps[:],
                    in1=xsb[:, tt * 1024 + (1 - c) * 512:tt * 1024 + (2 - c) * 512],
                    op=ALU.mult), tag + "vev")
        for c in range(2):
            t_wait_v()
            def mfc(e, c=c):
                e.matmul(pS1[0:4, :] if False else pG[0:4, c * 512:(c + 1) * 512],
                         lhsT=fcdb[:, 0:4], rhs=TC(xsb, 0, c), start=True, stop=False)
                return e.matmul(pG[0:4, c * 512:(c + 1) * 512], lhsT=fcdb[:, 4:8],
                                rhs=TC(xsb, 1, c), start=False, stop=True)
            E.op("tensor", mfc, tag + "mmv")
            v_wait_t()
            E.sop("vector", lambda e, c=c: e.tensor_scalar(
                out=CH(r4, c), in0=pG[0:4, c * 512:(c + 1) * 512],
                scalar1=dvec[0:4, :], scalar2=None, op0=ALU.add), tag + "vev")
        E.sop("vector", lambda e: e.tensor_copy(out=r4b[0:4, :], in_=r4[0:4, :]),
              tag + "vev")
        for tt in range(2):
            for c in range(2):
                ps = pC if c == 0 else pD2
                t_wait_v()
                E.op("tensor", lambda e, tt=tt, c=c, ps=ps: e.matmul(
                    ps[:], lhsT=e4b[:, tt * 128:(tt + 1) * 128], rhs=CH(r4b, c),
                    start=True, stop=True), tag + "mmv")
                v_wait_t()
                E.sop("vector", lambda e, tt=tt, c=c, ps=ps: e.tensor_tensor(
                    out=CH(scr, 0), in0=ps[:], in1=TC(xsb, tt, c), op=ALU.mult))
                E.sop("vector", lambda e, tt=tt, c=c: e.tensor_tensor(
                    out=TC(yt, tt, c), in0=TC(yt, tt, c), in1=CH(scr, 0),
                    op=ALU.add), tag + "vev")
        for tt in range(2):
            E.wait("scalar", tag + "vev", E.val(tag + "vev"))
            E.op("scalar", lambda e, tt=tt: e.activation(
                out=scr[:], in_=TL(zx, tt), func=AF.Silu), tag + "slz")
            E.wait("vector", tag + "slz", E.val(tag + "slz"))
            E.sop("vector", lambda e, tt=tt: e.tensor_tensor(
                out=TL(yt, tt), in0=TL(yt, tt), in1=scr[:], op=ALU.mult), tag + "yz")
            E.wait("scalar", tag + "yz", E.val(tag + "yz"))
            E.op("scalar", lambda e, tt=tt: e.activation(
                out=sqz[:, tt * 1024:(tt + 1) * 1024], in_=TL(yt, tt),
                func=AF.Square), tag + "sqz")
        E.wait("tensor", tag + "sqz", E.val(tag + "sqz"))
        for c in range(2):
            t_wait_v()
            E.wait("tensor", tag + "ms1", E.val(tag + "ms1"))
            def mms2(e, c=c):
                e.matmul(pS2[:], lhsT=ones_bf[:], rhs=sqz[:, c * 512:(c + 1) * 512],
                         start=True, stop=False)
                return e.matmul(pS2[:], lhsT=ones_bf[:],
                                rhs=sqz[:, 1024 + c * 512:1024 + (c + 1) * 512],
                                start=False, stop=True)
            E.op("tensor", mms2, tag + "mmv")
            v_wait_t()
            E.sop("vector", lambda e, c=c: e.tensor_scalar(
                out=CH(msr, c), in0=pS2[:], scalar1=1.0 / 256.0, scalar2=EPS,
                op0=ALU.mult, op1=ALU.add), tag + "ms1")
        E.wait("scalar", tag + "ms1", E.val(tag + "ms1"))
        E.op("scalar", lambda e: e.activation(out=msr[:], in_=msr[:], func=AF.Sqrt),
             tag + "ms2")
        E.wait("vector", tag + "ms2", E.val(tag + "ms2"))
        E.sop("vector", lambda e: e.reciprocal(out=msr[:], in_=msr[:]), tag + "vev")
        for c in range(2):
            t_wait_v()
            E.op("tensor", lambda e, c=c: e.matmul(
                pC[:], lhsT=o1[:], rhs=CH(msr, c), start=True, stop=True), tag + "mmv")
            v_wait_t()
            for tt in range(2):
                E.sop("vector", lambda e, tt=tt, c=c: e.tensor_tensor(
                    out=CH(scr, 0), in0=pC[:], in1=TC(yt, tt, c), op=ALU.mult))
                E.sop("vector", lambda e, tt=tt, c=c: e.tensor_scalar(
                    out=TC(ynb, tt, c), in0=CH(scr, 0), scalar1=nw[:, tt:tt + 1],
                    scalar2=None, op0=ALU.mult), tag + "vev")
        for c in range(2):
            ps = pC if c == 0 else pD2
            t_wait_v()
            def mop(e, c=c, ps=ps):
                e.matmul(ps[:], lhsT=outwb[:, 0:128], rhs=TC(ynb, 0, c),
                         start=True, stop=False)
                return e.matmul(ps[:], lhsT=outwb[:, 128:256], rhs=TC(ynb, 1, c),
                                start=False, stop=True)
            E.op("tensor", mop, tag + "mmv")
            v_wait_t()
            E.sop("vector", lambda e, c=c, ps=ps: e.tensor_copy(
                out=CH(un, c), in_=ps[:]), tag + "vev")
        if call == 0:
            E.sop("vector", lambda e: e.tensor_copy(out=ub[:], in_=un[:]), tag + "vev")

    E.sop("vector", lambda e: e.tensor_scalar(
        out=scr[:], in0=u0[:], scalar1=alv[:, 2:3], scalar2=None, op0=ALU.mult))
    E.sop("vector", lambda e: e.scalar_tensor_tensor(
        out=scr[:], in0=un[:], scalar=alv[:, 1:2], in1=scr[:],
        op0=ALU.mult, op1=ALU.add))
    E.sop("vector", lambda e: e.tensor_copy(out=puc[:], in_=scr[:]), tag + "vev")

    SL = lambda i: stv[0:1, i * 512:(i + 1) * 512]
    for tc in range(2):
        for k in range(nk):
            ps = pA if k % 2 == 0 else pB
            n = tc * nk + k
            if n >= 2:
                E.wait("tensor", tag + "ep", n - 1)
            E.wait("tensor", tag + "vev", E.val(tag + "vev"))
            E.op("tensor", lambda e, k=k, tc=tc, ps=ps: e.matmul(
                ps[:], lhsT=wupb[:, k * 128:(k + 1) * 128], rhs=CH(puc, tc),
                start=True, stop=True), tag + "umm")
            E.wait("vector", tag + "umm", E.val(tag + "umm"))
            E.sop("vector", lambda e, k=k, ps=ps: e.tensor_scalar(
                out=t1_sb[:], in0=ps[:], scalar1=bup[:, k:k + 1], scalar2=None,
                op0=ALU.add))
            E.sop("vector", lambda e, k=k, tc=tc: e.scalar_tensor_tensor(
                out=KC(x2_sb, k, tc), in0=KC(x2_sb, k, tc), scalar=2.0,
                in1=t1_sb[:], op0=ALU.mult, op1=ALU.add), tag + "ep")
            E.wait("scalar", tag + "ep", E.val(tag + "ep"))
            E.op("scalar", lambda e, k=k, tc=tc: e.activation(
                out=sq_sb[:], in_=KC(x2_sb, k, tc), func=AF.Square), tag + "sq")
            E.wait("tensor", tag + "sq", E.val(tag + "sq"))
            if tc == 1 and k == 0:
                E.wait("tensor", tag + "f1", E.val(tag + "f1"))
            def mst(e, k=k, tc=tc):
                e.matmul(pS1[:], lhsT=ones_bf[:], rhs=KC(x2_sb, k, tc),
                         start=(k == 0), stop=(k == nk - 1))
                return e.matmul(pS2[:], lhsT=ones_bf[:], rhs=sq_sb[:],
                                start=(k == 0), stop=(k == nk - 1))
            E.op("tensor", mst, tag + "stat")
        E.wait("vector", tag + "stat", E.val(tag + "stat"))
        E.sop("vector", lambda e: e.tensor_scalar(out=SL(0), in0=pS1[:],
              scalar1=1.0 / HID, scalar2=None, op0=ALU.mult))
        E.sop("vector", lambda e: e.tensor_scalar(out=SL(1), in0=pS2[:],
              scalar1=1.0 / HID, scalar2=None, op0=ALU.mult))
        E.sop("vector", lambda e: e.tensor_tensor(out=SL(3), in0=SL(0), in1=SL(0),
              op=ALU.mult))
        E.sop("vector", lambda e: e.tensor_tensor(out=SL(2), in0=SL(1), in1=SL(3),
              op=ALU.subtract))
        E.sop("vector", lambda e: e.tensor_scalar(out=SL(3), in0=SL(2),
              scalar1=EPS, scalar2=None, op0=ALU.add), tag + "f1")
        E.wait("scalar", tag + "f1", E.val(tag + "f1"))
        E.op("scalar", lambda e: e.activation(out=SL(2), in_=SL(3), func=AF.Sqrt),
             tag + "f2")
        E.wait("vector", tag + "f2", E.val(tag + "f2"))
        E.sop("vector", lambda e: e.reciprocal(out=SL(4), in_=SL(2)))
        E.sop("vector", lambda e: e.tensor_tensor(out=SL(3), in0=SL(0), in1=SL(4),
              op=ALU.mult))
        E.sop("vector", lambda e: e.tensor_scalar(out=SL(5), in0=SL(3),
              scalar1=-1.0, scalar2=None, op0=ALU.mult), tag + "f3")
        E.wait("tensor", tag + "f3", E.val(tag + "f3"))
        E.op("tensor", lambda e: e.matmul(pC[:], lhsT=o1[:], rhs=SL(4),
             start=True, stop=True))
        E.op("tensor", lambda e: e.matmul(pD2[:], lhsT=o1[:], rhs=SL(5),
             start=True, stop=True), tag + "f4")
        E.wait("vector", tag + "f4", E.val(tag + "f4"))
        E.sop("vector", lambda e: e.tensor_copy(out=ab_sb[:], in_=pC[:]))
        E.sop("vector", lambda e: e.tensor_copy(out=bb_sb[:], in_=pD2[:]), tag + "f5")
        for k in range(nk):
            bb = k % 2
            n = tc * nk + k
            if n >= 2:
                E.wait("vector", tag + f"out{bb}", 16 * ((n - 2) // 2 + 1))
            E.sop("vector", lambda e, k=k, tc=tc: e.tensor_tensor(
                out=t1_sb[:], in0=KC(x2_sb, k, tc), in1=ab_sb[:], op=ALU.mult))
            E.sop("vector", lambda e, bb=bb: e.tensor_tensor(
                out=out_sc[bb][:], in0=t1_sb[:], in1=bb_sb[:], op=ALU.add),
                tag + "ln")
            E.wait("gpsimd", tag + "ln", E.val(tag + "ln"))
            E.dma("gpsimd", xout[k * 128:(k + 1) * 128, tc * 512:(tc + 1) * 512],
                  out_sc[bb][:], tag + f"out{bb}")
        if tc == 0:
            E.wait("tensor", tag + "f5", E.val(tag + "f5"))
    for bb2 in range(2):
        E.wait("gpsimd", tag + f"out{bb2}", E.val(tag + f"out{bb2}"))
        E.wait("sync", tag + f"out{bb2}", E.val(tag + f"out{bb2}"))
    return p


def prep_l1_attn(d):
    inw = d["L1_attn_hy_in_w"].astype(np.float32)
    A6 = np.zeros((6, 128, 128), np.float32)
    rows = np.zeros((6, 128), np.int64) - 1
    A6[0, :, :] = inw[0:128]
    A6[1, :, :] = inw[128:256]
    A6[2, :, :] = inw[256:384]
    A6[3, :, :] = inw[384:512]
    A6[4, 0:16, :] = inw[512:528]
    A6[4, 32:48, :] = inw[528:544]
    A6[5, 0:4, :] = inw[544:548]
    A6[5, 32:36, :] = inw[548:552]
    inw_pay = np.ascontiguousarray(A6.transpose(2, 0, 1).reshape(128, 768))
    cw = d["L1_attn_hy_conv_w"].astype(np.float32)[:, 0, :]
    cb = d["L1_attn_hy_conv_b"].astype(np.float32)
    cvec = np.zeros((6, 128, 3), np.float32)
    for tt, t in ((0, 2), (1, 3)):
        cvec[t, :, 0] = cw[tt * 128:(tt + 1) * 128, 5]
        cvec[t, :, 1] = cw[tt * 128:(tt + 1) * 128, 6]
        cvec[t, :, 2] = cb[tt * 128:(tt + 1) * 128]
    cvec[4, 0:16, 0] = cw[256:272, 5]; cvec[4, 0:16, 1] = cw[256:272, 6]
    cvec[4, 0:16, 2] = cb[256:272]
    cvec[4, 32:48, 0] = cw[272:288, 5]; cvec[4, 32:48, 1] = cw[272:288, 6]
    cvec[4, 32:48, 2] = cb[272:288]
    cvec_pay = np.ascontiguousarray(cvec.transpose(1, 0, 2).reshape(128, 18))
    cvec_pay = np.concatenate([cvec_pay, np.zeros((128, 6), np.float32)], axis=1)
    outw = d["L1_attn_hy_out_w"].astype(np.float32)
    outw_pay = np.zeros((128, 256), np.float32)
    for tt in range(2):
        outw_pay[:, tt * 128:(tt + 1) * 128] = outw[:, tt * 128:(tt + 1) * 128].T
    fcd = d["L1_attn_hy_fcD_w"].astype(np.float32)
    fcd_pay = np.zeros((128, 8), np.float32)
    for tt in range(2):
        fcd_pay[:, tt * 4:(tt + 1) * 4] = fcd[:, tt * 128:(tt + 1) * 128].T
    e4 = np.zeros((4, 256), np.float32)
    for c in range(256):
        e4[c // 64, c] = 1.0
    wdn = d["L1_attn_w_down"].astype(np.float32)
    wdnT = wdn.T[PI]
    wdn_pay = np.ascontiguousarray(
        wdnT.reshape(32, 128, 128).transpose(1, 0, 2).reshape(128, 4096))
    wup = d["L1_attn_w_up"].astype(np.float32)
    wup_pay = np.ascontiguousarray(wup[PI].T)
    bup = d["L1_attn_b_up"].astype(np.float32)[PI]
    bup_pay = np.ascontiguousarray(bup.reshape(32, 128).T)
    nw = d["L1_attn_hy_norm_w"].astype(np.float32)
    nw_pay = np.ascontiguousarray(nw.reshape(2, 128).T)
    return dict(
        hawdn=wdn_pay, habdn=d["L1_attn_b_down"].astype(np.float32).reshape(128, 1),
        hawup=wup_pay, habup=bup_pay, hainw=inw_pay, hacvec=cvec_pay,
        hadtb=np.concatenate([d["L1_attn_hy_dt_bias"].astype(np.float32).reshape(4, 1),
                              np.zeros((28, 1), np.float32),
                              d["L1_attn_hy_dt_bias"].astype(np.float32).reshape(4, 1)]),
        hadvec=d["L1_attn_hy_D"].astype(np.float32).reshape(4, 1),
        hafcd=fcd_pay, hanw=nw_pay, haoutw=outw_pay,
        hagate=d["L1_attn_gate1"].astype(np.float32).reshape(1, 1),
        hao16=np.ones((16, 1), np.float32), hao1=np.ones((1, 128), np.float32),
        hae4=e4, haonesb=np.ones((128, 1), BF16NP),
    )


def build_att_a(p, E, tag="aa"):
    xb_in = p.din(tag + "xb", [4096, 4096], F32)
    xmy_in = p.din(tag + "xmy", [1024, 4096], F32)
    wkv_in = p.din(tag + "wkv", [128, 32 * 4 * 128], F32)
    wq_in = p.din(tag + "wq", [32, 128, 32 * 128], F32)
    bkv_in = p.din(tag + "bkv", [128, 4], F32)
    bq_in = p.din(tag + "bq", [128, 32], F32)
    ckv_in = p.din(tag + "ckv", [128, 4096], F32)
    skv_in = p.din(tag + "skv", [128, 4096], F32)
    cq_in = p.din(tag + "cq", [128, 1024], F32)
    sq_in = p.din(tag + "sq", [128, 1024], F32)

    xbf_my = p.dout(tag + "xbfmy", [1024, 4096], BF16)
    kv_o = p.dout(tag + "kvo", [4, 128, 4096], BF16)
    qt_o = p.dout(tag + "qto", [32, 128, 1024], BF16)
    xbf_b = p.dint(tag + "xbfb", [4096, 4096], BF16)

    cast_sb = [p.sbuf(tag + f"cs{i}", [128, 4096], BF16) for i in range(2)]
    wkvb = p.sbuf(tag + "wkvb", [128, 32 * 4 * 128], BF16)
    wqb = [p.sbuf(tag + f"wqb{i}", [128, 32 * 128], BF16) for i in range(2)]
    bkv = p.sbuf(tag + "bkvs", [128, 4], F32)
    bq = p.sbuf(tag + "bqs", [128, 32], F32)
    ckv = p.sbuf(tag + "ckvs", [128, 4096], F32)
    skv = p.sbuf(tag + "skvs", [128, 4096], F32)
    cq = p.sbuf(tag + "cqs", [128, 1024], F32)
    sq_ = p.sbuf(tag + "sqs", [128, 1024], F32)
    xtb = [p.sbuf(tag + f"xtb{i}", [128, 32 * 512], BF16) for i in range(2)]
    ra = p.sbuf(tag + "ra", [128, 512], F32)
    rb = p.sbuf(tag + "rb", [128, 512], F32)
    rc = p.sbuf(tag + "rc", [128, 512], F32)
    rcs = p.sbuf(tag + "rcs", [128, 512], F32)
    outb = [p.sbuf(tag + f"ob{i}", [128, 512], BF16) for i in range(2)]
    pk = [p.psum(tag + f"pk{i}", [128, 512]) for i in range(2)]

    for dst, src in ((bkv, bkv_in), (bq, bq_in), (ckv, ckv_in), (skv, skv_in),
                     (cq, cq_in), (sq_, sq_in)):
        E.dma("sync", dst[:], src, tag + "c")
    E.dma("gpsimd", wkvb[:], wkv_in, tag + "cg")

    njobs = [(xmy_in, xbf_my, i) for i in range(8)]
    njobs += [(xb_in, xbf_b, i) for i in range(32)]
    for n, (src, dst, i) in enumerate(njobs):
        b = n % 2
        if n >= 2:
            E.wait("gpsimd", tag + f"cst{b}", 16 * (n // 2))
        E.dma("gpsimd", cast_sb[b][:], src[i * 128:(i + 1) * 128, :], tag + f"cl{b}")
        E.wait("sync", tag + f"cl{b}")
        E.dma("sync", dst[i * 128:(i + 1) * 128, :], cast_sb[b][:], tag + f"cst{b}")
    E.wait("sync", tag + "cst0", E.val(tag + "cst0"))
    E.wait("sync", tag + "cst1", E.val(tag + "cst1"))
    E.wait("vector", tag + "c", E.val(tag + "c"))
    E.wait("tensor", tag + "cg", E.val(tag + "cg"))

    def rope_store(ps, bias_col, C, S, csl, dst, nsem):
        ob = outb[nsem % 2]
        if nsem >= 2:
            E.wait("vector", tag + f"ost{nsem % 2}", 16 * (nsem // 2))
        E.sop("vector", lambda e: e.tensor_scalar(
            out=ra[:], in0=ps[:], scalar1=bias_col, scalar2=None, op0=ALU.add))
        E.sop("vector", lambda e: e.tensor_tensor(
            out=rb[:], in0=ra[:], in1=C[:, csl], op=ALU.mult))
        E.sop("vector", lambda e: e.tensor_tensor(
            out=rc[:], in0=ra[:], in1=S[:, csl], op=ALU.mult))
        E.sop("vector", lambda e: e.tensor_copy(out=rcs[0:64, :], in_=rc[64:128, :]))
        E.sop("vector", lambda e: e.tensor_copy(out=rcs[64:128, :], in_=rc[0:64, :]))
        E.sop("vector", lambda e: e.tensor_tensor(
            out=ob[0:64, :], in0=rb[0:64, :], in1=rcs[0:64, :], op=ALU.subtract))
        E.sop("vector", lambda e: e.tensor_tensor(
            out=ob[64:128, :], in0=rb[64:128, :], in1=rcs[64:128, :], op=ALU.add),
            tag + "rope")
        E.wait("gpsimd", tag + "rope", E.val(tag + "rope"))
        E.dma("gpsimd", dst, ob[:], tag + f"ost{nsem % 2}")

    nst = 0
    for t8 in range(8):
        xb2 = t8 % 2
        if t8 >= 2:
            E.wait("sync", tag + "kvmm", (t8 - 1) * 4)
        for k in range(NKT):
            E.dma("sync", xtb[xb2][:, k * 512:(k + 1) * 512],
                  xbf_b[t8 * 512:(t8 + 1) * 512, k * 128:(k + 1) * 128],
                  tag + f"xt{xb2}", transpose=True)
        E.wait("tensor", tag + f"xt{xb2}")
        for j in range(4):
            ps = pk[j % 2]
            if t8 * 4 + j >= 2:
                E.wait("tensor", tag + "rope", t8 * 4 + j - 1)
            def mkv(e, j=j, ps=ps, xb2=xb2):
                for k in range(NKT - 1):
                    e.matmul(ps[:], lhsT=wkvb[:, (k * 4 + j) * 128:(k * 4 + j + 1) * 128],
                             rhs=xtb[xb2][:, k * 512:(k + 1) * 512],
                             start=(k == 0), stop=False)
                return e.matmul(ps[:], lhsT=wkvb[:, ((NKT - 1) * 4 + j) * 128:((NKT - 1) * 4 + j + 1) * 128],
                                rhs=xtb[xb2][:, (NKT - 1) * 512:NKT * 512],
                                start=False, stop=True)
            E.op("tensor", mkv, tag + "kvmm")
            E.wait("vector", tag + "kvmm", E.val(tag + "kvmm"))
            rope_store(ps, bkv[:, j:j + 1], ckv, skv, slice(t8 * 512, (t8 + 1) * 512),
                       kv_o[j][:, t8 * 512:(t8 + 1) * 512], nst)
            nst += 1
    for tq in range(2):
        xb2 = tq % 2
        E.wait("sync", tag + "kvmm", E.val(tag + "kvmm"))
        for k in range(NKT):
            E.dma("sync", xtb[xb2][:, k * 512:(k + 1) * 512],
                  xbf_my[tq * 512:(tq + 1) * 512, k * 128:(k + 1) * 128],
                  tag + f"xq{xb2}", transpose=True)
        E.wait("tensor", tag + f"xq{xb2}")
        for hb in range(32):
            wb_ = hb % 2
            nq = tq * 32 + hb
            if nq >= 2:
                E.wait("gpsimd", tag + "qmm", nq - 1)
            E.dma("gpsimd", wqb[wb_][:], wq_in[hb], tag + f"wq{wb_}")
            E.wait("tensor", tag + f"wq{wb_}")
            ps = pk[hb % 2]
            E.wait("tensor", tag + "rope", 32 + nq - 1)
            def mq(e, ps=ps, wb_=wb_, xb2=xb2):
                for k in range(NKT - 1):
                    e.matmul(ps[:], lhsT=wqb[wb_][:, k * 128:(k + 1) * 128],
                             rhs=xtb[xb2][:, k * 512:(k + 1) * 512],
                             start=(k == 0), stop=False)
                return e.matmul(ps[:], lhsT=wqb[wb_][:, (NKT - 1) * 128:NKT * 128],
                                rhs=xtb[xb2][:, (NKT - 1) * 512:NKT * 512],
                                start=False, stop=True)
            E.op("tensor", mq, tag + "qmm")
            E.wait("vector", tag + "qmm", E.val(tag + "qmm"))
            rope_store(ps, bq[:, hb:hb + 1], cq, sq_, slice(tq * 512, (tq + 1) * 512),
                       qt_o[hb][:, tq * 512:(tq + 1) * 512], nst)
            nst += 1
    for b2 in range(2):
        E.wait("gpsimd", tag + f"ost{b2}", E.val(tag + f"ost{b2}"))
        E.wait("sync", tag + f"ost{b2}", E.val(tag + f"ost{b2}"))
    return p


def prep_att_a(d, rope_tabs):
    cosf, sinf = rope_tabs
    wq = d["L0_attn_Wq"].astype(np.float32)[PI]
    wq_swz = np.ascontiguousarray(
        wq.reshape(32, 128, 32, 128).transpose(2, 1, 0, 3).reshape(32, 128, 4096))
    wk = d["L0_attn_Wk"].astype(np.float32)[PI]
    wv = d["L0_attn_Wv"].astype(np.float32)[PI]
    wkv = np.concatenate([wk, wv], axis=1)
    bkv_full = np.concatenate([d["L0_attn_bk"], d["L0_attn_bv"]]).astype(np.float32)
    bq_full = d["L0_attn_bq"].astype(np.float32)
    maps = []
    ident_c = np.ones((4096, 64), np.float32)
    ident_s = np.zeros((4096, 64), np.float32)
    for r in range(NC):
        j4 = r % 4
        cols = slice(512 * j4, 512 * (j4 + 1))
        wslice = wkv[:, cols]
        wkv_swz = np.ascontiguousarray(
            wslice.reshape(32, 128, 4, 128).transpose(1, 0, 2, 3).reshape(128, 16384))
        bkv_sl = np.ascontiguousarray(bkv_full[cols].reshape(4, 128).T)
        is_k = j4 < 2
        ckv = (cosf if is_k else ident_c)
        skv = (sinf if is_k else ident_s)
        ckv_t = np.ascontiguousarray(np.concatenate([ckv.T, ckv.T], axis=0))
        skv_t = np.ascontiguousarray(np.concatenate([skv.T, skv.T], axis=0))
        pos = np.concatenate([np.arange((8 * j + r) * 128, (8 * j + r + 1) * 128)
                              for b in range(2) for j in range(4)])
        cq_t = np.ascontiguousarray(np.concatenate([cosf[pos].T, cosf[pos].T], axis=0))
        sq_t = np.ascontiguousarray(np.concatenate([sinf[pos].T, sinf[pos].T], axis=0))
        maps.append(dict(
            aawkv=wkv_swz, aawq=wq_swz, aabkv=bkv_sl,
            aabq=np.ascontiguousarray(bq_full[:].reshape(32, 128).T),
            aackv=ckv_t, aaskv=skv_t, aacq=cq_t, aasq=sq_t,
        ))
    return maps


def rope_tables():
    half = 64
    inv_freq = 1.0 / (10000.0 ** (np.arange(0, half, dtype=np.float32) / half))
    ang = np.arange(S, dtype=np.float32)[:, None] * inv_freq[None, :]
    return np.cos(ang).astype(np.float32), np.sin(ang).astype(np.float32)


def build_att_b(p, E, tag="ab"):
    qt_in = p.din(tag + "qt", [32, 128, 1024], BF16)
    kt_in = p.din(tag + "kt", [2, 1024, 4096], BF16)
    vt_in = p.din(tag + "vt", [2, 8, 128, 4096], BF16)
    xbfmy_in = p.din(tag + "xbfmy", [1024, 4096], BF16)
    wo_in = p.din(tag + "wo", [32, 128, 4096], F32)
    bo_in = p.din(tag + "bo", [128, 32], F32)
    nmask_in = p.din(tag + "nmask", [128, 8 * 512], BF16)
    emask_in = p.din(tag + "emask", [128, 8 * 128], BF16)
    ident_in = p.din(tag + "ident", [128, 128], F32)
    o1_in = p.din(tag + "o1", [1, 128], F32)
    x1_out = p.dout(tag + "x1", [4096, 1024], BF16)

    kt_sb = p.sbuf(tag + "kts", [128, 4096], BF16)
    vb_sb = p.sbuf(tag + "vbs", [128, 4096], BF16)
    qsb = [p.sbuf(tag + f"qs{i}", [128, 512], BF16) for i in range(2)]
    ctx_all = [p.sbuf(tag + f"ctx{i}", [128, 32 * 512], BF16) for i in range(2)]
    xres = p.sbuf(tag + "xres", [128, 32 * 1024], BF16)
    nmask = p.sbuf(tag + "nmk", [128, 8 * 512], BF16)
    emask = p.sbuf(tag + "emk", [128, 8 * 128], BF16)
    ident = p.sbuf(tag + "id", [128, 128], F32)
    o1 = p.sbuf(tag + "o1s", [1, 128], F32)
    bo = p.sbuf(tag + "bos", [128, 32], F32)
    wob = [p.sbuf(tag + f"wob{i}", [128, 4096], BF16) for i in range(2)]
    mall = p.sbuf(tag + "mall", [128, 4], F32)
    negm = p.sbuf(tag + "negm", [128, 4], F32)
    zbuf = p.sbuf(tag + "zbuf", [128, 20], F32)
    zs = p.sbuf(tag + "zs", [128, 4], F32)
    rec = p.sbuf(tag + "rec", [128, 4], F32)
    rho4 = p.sbuf(tag + "rho4", [128, 4], F32)
    knc = p.sbuf(tag + "knc", [1, 16], F32)
    mh = p.sbuf(tag + "mh", [1, 1], F32)
    mhb = p.sbuf(tag + "mhb", [128, 1], F32)
    nmh = p.sbuf(tag + "nmh", [128, 1], F32)
    rrow = p.sbuf(tag + "rrow", [1, 512], F32)
    rbs = p.sbuf(tag + "rbs", [128, 512], F32)
    escr = p.sbuf(tag + "escr", [128, 1024], BF16)
    et = [p.sbuf(tag + f"et{i}", [128, 512], BF16) for i in range(4)]
    x0t = [p.sbuf(tag + f"x0t{i}", [128, 512], BF16) for i in range(2)]
    sq_sb = [p.sbuf(tag + f"sqb{i}", [128, 512], BF16) for i in range(2)]
    stv = p.sbuf(tag + "stv", [1, 512 * 6], F32)
    mqn = stv[0:1, 2560:3072]
    ab_sb = p.sbuf(tag + "abs", [128, 512], F32)
    bb_sb = p.sbuf(tag + "bbs", [128, 512], F32)
    t1_sb = p.sbuf(tag + "t1s", [128, 512], F32)
    out_sc = [p.sbuf(tag + f"osc{i}", [128, 512], BF16) for i in range(2)]
    ones_bf = p.sbuf(tag + "onb", [128, 1], BF16)
    onesb_in = p.din(tag + "onb_in", [128, 1], BF16)

    pn = [p.psum(tag + f"pn{i}", [128, 1024]) for i in range(2)]
    pt = [p.psum(tag + f"pt{i}", [128, 512]) for i in range(2)]
    pcx = [p.psum(tag + f"pcx{i}", [128, 512]) for i in range(2)]

    E.dma("sync", ident[:], ident_in, tag + "c")
    E.dma("sync", o1[:], o1_in, tag + "c")
    E.dma("sync", bo[:], bo_in, tag + "c")
    E.dma("sync", ones_bf[:], onesb_in, tag + "c")
    E.dma("sync", emask[:], emask_in, tag + "c")
    E.dma("sync", nmask[:], nmask_in, tag + "c")
    E.wait("vector", tag + "c", E.val(tag + "c"))
    E.wait("tensor", tag + "c", E.val(tag + "c"))
    E.wait("scalar", tag + "c", E.val(tag + "c"))

    NM = lambda j, i: nmask[:, (j * 2 + i) * 512:(j * 2 + i + 1) * 512]
    EM = lambda i: emask[:, i * 128:(i + 1) * 128]
    chunks_of = lambda j: 2 * (j + 1)

    for b in range(2):
        for g in range(8):
            u = b * 8 + g
            if u > 0:
                E.wait("sync", tag + "cmm", 32 * u * 4)
            E.dma("sync", qsb[(u * 4) % 2][:],
                  qt_in[g * 4][:, b * 512:(b + 1) * 512], tag + f"ql{(u * 4) % 2}")
            for cc in range(8):
                E.dma("sync", kt_sb[:, cc * 512:(cc + 1) * 512],
                      kt_in[b][g * 128:(g + 1) * 128, cc * 512:(cc + 1) * 512],
                      tag + f"ktc{cc}")
            if u > 0:
                E.wait("scalar", tag + "cmm", 32 * u * 4)
            E.dma("scalar", vb_sb[:], vt_in[b][g], tag + "vtl")
            vtv = E.val(tag + "vtl")
            E.wait("scalar", tag + "zred", E.val(tag + "zred"))
            E.wait("tensor", tag + "eexp", E.val(tag + "eexp"))
            for cc in range(8):
                eh = (cc % 2) * 512
                if cc >= 2:
                    E.wait("scalar", tag + "knmm", E.val(tag + "knmm") - 1)
                E.wait("scalar", tag + f"ktc{cc}", E.val(tag + f"ktc{cc}"))
                E.op("scalar", lambda e, cc=cc, eh=eh: e.activation(
                    out=escr[:, eh:eh + 512], in_=kt_sb[:, cc * 512:(cc + 1) * 512],
                    func=AF.Square), tag + "knsq")
                E.wait("tensor", tag + "knsq", E.val(tag + "knsq"))
                if cc >= 2:
                    E.wait("tensor", tag + "knred", E.val(tag + "knred") - 1)
                E.op("tensor", lambda e, cc=cc, eh=eh: e.matmul(
                    pt[cc % 2][0:1, 0:512], lhsT=ones_bf[:], rhs=escr[:, eh:eh + 512],
                    start=True, stop=True), tag + "knmm")
                E.wait("vector", tag + "knmm", E.val(tag + "knmm"))
                E.sop("vector", lambda e, cc=cc: e.reduce_max(
                    out=knc[0:1, cc:cc + 1], in_=pt[cc % 2][0:1, 0:512], axis=AX.X),
                    tag + "knred")
            E.sop("vector", lambda e: e.reduce_max(out=knc[0:1, 8:9],
                  in_=knc[0:1, 0:8], axis=AX.X), tag + "knr2")
            E.wait("scalar", tag + "knr2", E.val(tag + "knr2"))
            E.op("scalar", lambda e: e.activation(out=knc[0:1, 9:10],
                 in_=knc[0:1, 8:9], func=AF.Sqrt), tag + "knf")
            for h4 in range(4):
                h = g * 4 + h4
                nu = u * 4 + h4
                qb2 = nu % 2
                if h4 > 0:
                    if nu >= 2:
                        E.wait("sync", tag + "cmm", 32 * (nu - 1))
                    E.dma("sync", qsb[qb2][:], qt_in[h][:, b * 512:(b + 1) * 512],
                          tag + f"ql{qb2}")
                E.wait("tensor", tag + f"ql{qb2}")
                E.wait("scalar", tag + f"ql{qb2}", E.val(tag + f"ql{qb2}"))
                E.wait("scalar", tag + "qnm", E.val(tag + "qnm"))
                E.op("scalar", lambda e, qb2=qb2: e.activation(
                    out=sq_sb[0][:], in_=qsb[qb2][:], func=AF.Square), tag + "qsq")
                E.wait("tensor", tag + "qsq", E.val(tag + "qsq"))
                E.wait("tensor", tag + "ctxev", E.val(tag + "ctxev"))
                E.wait("tensor", tag + "rbsv", E.val(tag + "rbsv"))
                pcs = pcx[(nu + 1) % 2]
                E.op("tensor", lambda e, pcs=pcs: e.matmul(
                    pcs[0:1, 0:512], lhsT=ones_bf[:], rhs=sq_sb[0][:],
                    start=True, stop=True), tag + "qnm")
                E.wait("scalar", tag + "qnm", E.val(tag + "qnm"))
                E.wait("scalar", tag + "knf", E.val(tag + "knf"))
                E.op("scalar", lambda e, pcs=pcs: e.activation(
                    out=mqn, in_=pcs[0:1, 0:512], func=AF.Sqrt), tag + "qsr")
                E.wait("vector", tag + "qsr", E.val(tag + "qsr"))
                E.sop("vector", lambda e: e.tensor_scalar(
                    out=mqn, in0=mqn, scalar1=knc[0:1, 9:10], scalar2=INVSQD,
                    op0=ALU.mult, op1=ALU.mult), tag + "qts")
                E.sop("vector", lambda e: e.reduce_max(out=mh[:], in_=mqn,
                      axis=AX.X), tag + "vmh2")
                def hook_p0():
                    E.wait("tensor", tag + "qts", E.val(tag + "qts"))
                    for j in range(4):
                        E.op("tensor", lambda e, j=j, pcs=pcs: e.matmul(
                            pcs[:, 1 + j:2 + j], lhsT=mqn[:, j * 128:(j + 1) * 128],
                            rhs=o1[0:1, 0:1], start=True, stop=True), tag + "mtt")
                    E.wait("vector", tag + "mtt", E.val(tag + "mtt"))
                    E.sop("vector", lambda e, pcs=pcs: e.tensor_copy(
                        out=mall[:], in_=pcs[:, 1:5]), tag + "mtc")
                    E.sop("vector", lambda e: e.tensor_scalar(
                        out=negm[:], in0=mall[:], scalar1=-INVSQD, scalar2=None,
                        op0=ALU.mult), tag + "nfin")
                def hook_p1():
                    E.wait("tensor", tag + "vmh2", E.val(tag + "vmh2"))
                    E.op("tensor", lambda e, pcs=pcs: e.matmul(pcs[:, 0:1], lhsT=o1[:],
                         rhs=mh[:], start=True, stop=True), tag + "bmh")
                    E.wait("vector", tag + "bmh", E.val(tag + "bmh"))
                    E.sop("vector", lambda e, pcs=pcs: e.tensor_copy(out=mhb[:],
                          in_=pcs[:, 0:1]))
                    E.sop("vector", lambda e: e.tensor_scalar(out=nmh[:], in0=mhb[:],
                          scalar1=-INVSQD, scalar2=None, op0=ALU.mult), tag + "vnmh")
                hooks = {0: hook_p0, 1: hook_p1}
                pos = 0
                for j in range(4):
                    for ch in range(chunks_of(j)):
                        half = pos % 2
                        pu_ = pn[(pos // 2) % 2]
                        if half == 0:
                            E.wait("tensor", tag + "zred", max(E.val(tag + "zred") - 1, 0))
                        E.op("tensor", lambda e, j=j, ch=ch, half=half, qb2=qb2, pu_=pu_: e.matmul(
                            pu_[:, half * 512:(half + 1) * 512],
                            lhsT=qsb[qb2][:, j * 128:(j + 1) * 128],
                            rhs=kt_sb[:, ch * 512:(ch + 1) * 512],
                            start=True, stop=True), tag + "zmm")
                        if half == 1:
                            hk = hooks.pop(pos // 2, None)
                            if hk:
                                hk()
                            if ch >= 2 * j:
                                E.wait("vector", tag + "zmm", E.val(tag + "zmm"))
                                E.sop("vector", lambda e, j=j, pu_=pu_: e.tensor_tensor(
                                    out=pu_[:, 0:1024], in0=pu_[:, 0:1024],
                                    in1=nmask[:, (j * 2) * 512:(j * 2 + 2) * 512],
                                    op=ALU.add), tag + "zmsk")
                                E.wait("scalar", tag + "zmsk", E.val(tag + "zmsk"))
                            else:
                                E.wait("scalar", tag + "zmm", E.val(tag + "zmm"))
                            E.wait("scalar", tag + "zred", E.val(tag + "zred"))
                            E.wait("scalar", tag + "nfin", E.val(tag + "nfin"))
                            E.op("scalar", lambda e, j=j, pos=pos, pu_=pu_: e.activation(
                                out=escr[:, 0:1024], in_=pu_[:, 0:1024], func=AF.Exp,
                                scale=INVSQD, bias=negm[:, j:j + 1],
                                accum_out=zbuf[:, pos // 2:pos // 2 + 1]), tag + "zred")
                        pos += 1
                    E.wait("vector", tag + "zred", E.val(tag + "zred"))
                    E.sop("vector", lambda e, j=j: e.reduce_sum(
                        out=zs[:, j:j + 1],
                        in_=zbuf[:, j * (j + 1) // 2:j * (j + 1) // 2 + j + 1], axis=AX.X))
                E.sop("vector", lambda e: e.tensor_scalar(out=rho4[:], in0=mall[:],
                      scalar1=mhb[:], scalar2=-INVSQD, op0=ALU.subtract,
                      op1=ALU.mult))
                E.sop("vector", lambda e: e.reciprocal(out=rec[:], in_=zs[:]),
                      tag + "vr1")
                E.wait("scalar", tag + "vr1", E.val(tag + "vr1"))
                E.op("scalar", lambda e: e.activation(out=rho4[:], in_=rho4[:],
                     func=AF.Exp), tag + "sr1")
                E.wait("vector", tag + "sr1", E.val(tag + "sr1"))
                E.sop("vector", lambda e: e.tensor_tensor(out=rho4[:], in0=rho4[:],
                      in1=rec[:], op=ALU.mult), tag + "vr2")
                def emit_rho(pcs=pcs):
                    for j in range(4):
                        E.wait("tensor", tag + "vr2", E.val(tag + "vr2"))
                        E.wait("tensor", tag + "rr", E.val(tag + "rr"))
                        E.op("tensor", lambda e, j=j, pcs=pcs: e.transpose(
                            pcs[0:1, 384:512], in_=rho4[:, j:j + 1], identity=ident[:]),
                            tag + "trr")
                        E.wait("vector", tag + "trr", E.val(tag + "trr"))
                        E.sop("vector", lambda e, j=j, pcs=pcs: e.tensor_copy(
                            out=rrow[0:1, j * 128:(j + 1) * 128], in_=pcs[0:1, 384:512]),
                            tag + "rr")
                    E.wait("tensor", tag + "rr", E.val(tag + "rr"))
                    E.op("tensor", lambda e, pcs=pcs: e.matmul(pcs[:], lhsT=o1[:],
                         rhs=rrow[:], start=True, stop=True), tag + "brr")
                    E.wait("vector", tag + "brr", E.val(tag + "brr"))
                    E.sop("vector", lambda e, pcs=pcs: e.tensor_copy(out=rbs[:],
                          in_=pcs[:]), tag + "rbsv")
                pct = pcx[nu % 2]
                eexp0 = E.val(tag + "eexp")
                emsk0 = E.val(tag + "emsk")
                cmm0 = E.val(tag + "cmm")
                def emit_mmT(gt):
                    ps = pt[gt % 2]
                    if gt >= 2:
                        E.wait("tensor", tag + "eexp", eexp0 + gt - 1)
                    E.op("tensor", lambda e, gt=gt, ps=ps, qb2=qb2: e.matmul(
                        ps[:], lhsT=kt_sb[:, gt * 128:(gt + 1) * 128],
                        rhs=qsb[qb2][:], start=True, stop=True), tag + "emm")
                    E.wait("scalar", tag + "emm", E.val(tag + "emm"))
                    jm = gt // 8
                    eb = et[gt % 4]
                    if gt == 0:
                        E.wait("scalar", tag + "vnmh", E.val(tag + "vnmh"))
                    if gt >= 4:
                        E.wait("scalar", tag + "cmm", cmm0 + gt - 3)
                    E.op("scalar", lambda e, gt=gt, jm=jm, ps=ps, eb=eb: e.activation(
                        out=eb[:, jm * 128:512], in_=ps[:, jm * 128:512],
                        func=AF.Exp, scale=INVSQD, bias=nmh[:]), tag + "eexp")
                    E.wait("gpsimd", tag + "eexp", E.val(tag + "eexp"))
                    pj = (gt - 4) // 8 if gt >= 4 else 0
                    if jm > pj:
                        E.sop("gpsimd", lambda e, pj=pj, jm=jm, eb=eb: e.memset(
                            eb[:, pj * 128:jm * 128], 0.0))
                    E.sop("gpsimd", lambda e, gt=gt, jm=jm, eb=eb: e.tensor_tensor(
                        out=eb[:, jm * 128:(jm + 1) * 128],
                        in0=eb[:, jm * 128:(jm + 1) * 128], in1=EM(gt % 8),
                        op=ALU.mult), tag + "emsk")
                def emit_ctx(gt):
                    eb = et[gt % 4]
                    E.wait("tensor", tag + "emsk", emsk0 + gt + 1)
                    if gt == 0:
                        E.wait("tensor", tag + "ctxev",
                               E.val(tag + "ctxev") - (1 if nu > 0 else 0))
                        E.wait("tensor", tag + "rbsv", E.val(tag + "rbsv"))
                    E.op("tensor", lambda e, gt=gt, eb=eb, pct=pct: e.matmul(
                        pct[:], lhsT=vb_sb[:, gt * 128:(gt + 1) * 128], rhs=eb[:],
                        start=(gt == 0), stop=(gt == NKT - 1)), tag + "cmm")
                if h4 == 0:
                    E.wait("tensor", tag + "vtl", vtv)
                emit_mmT(0)
                emit_mmT(1)
                for rr in range(1, NKT // 2):
                    emit_mmT(2 * rr)
                    emit_mmT(2 * rr + 1)
                    if rr == 8:
                        emit_rho()
                    emit_ctx(2 * rr - 2)
                    emit_ctx(2 * rr - 1)
                emit_ctx(NKT - 2)
                emit_ctx(NKT - 1)
                E.wait("vector", tag + "cmm", E.val(tag + "cmm"))
                E.wait("vector", tag + "rbsv", E.val(tag + "rbsv"))
                E.sop("vector", lambda e, b=b, h=h, pct=pct: e.tensor_tensor(
                    out=ctx_all[b][:, h * 512:(h + 1) * 512], in0=pct[:],
                    in1=rbs[:], op=ALU.mult), tag + "ctxev")
                if os.environ.get("ATTB_DEBUG") == "1" and nu == 0:
                    dbg = p.dout(tag + "dbg", [128, 16], F32)
                    dbg2 = p.dout(tag + "dbg2", [128, 1024], F32)
                    E.sop("vector", lambda e: e.tensor_copy(out=escr[:, 0:4], in_=mall[:]))
                    E.sop("vector", lambda e: e.tensor_copy(out=escr[:, 4:8], in_=zs[:]))
                    E.sop("vector", lambda e: e.tensor_copy(out=escr[:, 8:12], in_=rho4[:]))
                    E.sop("vector", lambda e: e.tensor_copy(out=escr[:, 12:13], in_=mhb[:]), tag + "dbgv")
                    E.wait("gpsimd", tag + "dbgv", E.val(tag + "dbgv"))
                    E.dma("gpsimd", dbg[:], escr[:, 0:16], tag + "dbgd")
                    E.dma("gpsimd", dbg2[:, 0:512], rbs[:], tag + "dbgd")
                    E.wait("vector", tag + "dbgd", E.val(tag + "dbgd"))

    if os.environ.get("ATTB_DEBUG") == "1":
        ctxd = p.dout(tag + "ctxd", [2, 128, 32 * 512], BF16)
        E.wait("gpsimd", tag + "ctxev", E.val(tag + "ctxev"))
        for b2 in range(2):
            E.dma("gpsimd", ctxd[b2], ctx_all[b2][:], tag + "ctxd")
        E.wait("gpsimd", tag + "ctxd", E.val(tag + "ctxd"))

    SL = lambda i: stv[0:1, i * 512:(i + 1) * 512]
    stats_sum = [pt[0][0:1, :], pt[1][0:1, :]]
    stats_sq = [pcx[0][0:1, :], pcx[1][0:1, :]]
    ctxev_all = E.val(tag + "ctxev")
    E.wait("tensor", tag + "ctxev", ctxev_all)
    E.wait("gpsimd", tag + "ctxev", ctxev_all)
    stat_pend = []
    for k in range(NKT):
        wb2 = k % 2
        if k >= 2:
            E.wait("gpsimd", tag + "omm", 2 * (k - 1))
        E.dma("gpsimd", wob[wb2][:], wo_in[k], tag + f"wol{wb2}")
        E.wait("tensor", tag + f"wol{wb2}")
        for b in range(2):
            po = pn[b]
            n = k * 2 + b
            if n >= 2:
                E.wait("tensor", tag + "oev", n - 1)
            def mo(e, b=b, po=po, wb2=wb2):
                for hh in range(31):
                    e.matmul(po[:, 0:512], lhsT=wob[wb2][:, hh * 128:(hh + 1) * 128],
                             rhs=ctx_all[b][:, hh * 512:(hh + 1) * 512],
                             start=(hh == 0), stop=False)
                return e.matmul(po[:, 0:512], lhsT=wob[wb2][:, 31 * 128:32 * 128],
                                rhs=ctx_all[b][:, 31 * 512:32 * 512],
                                start=False, stop=True)
            E.op("tensor", mo, tag + "omm")
            if stat_pend:
                stat_pend.pop()()
            E.wait("sync", tag + "oev", max(n - 1, 0))
            E.dma("sync", x0t[n % 2][:], xbfmy_in[b * 512:(b + 1) * 512,
                                                  k * 128:(k + 1) * 128],
                  tag + f"x0l{n % 2}", transpose=True)
            E.wait("vector", tag + "omm", E.val(tag + "omm"))
            E.wait("vector", tag + f"x0l{n % 2}", E.val(tag + f"x0l{n % 2}"))
            E.sop("vector", lambda e, k=k, po=po: e.tensor_scalar(
                out=t1_sb[:], in0=po[:, 0:512], scalar1=bo[:, k:k + 1], scalar2=None,
                op0=ALU.add))
            E.sop("vector", lambda e, k=k, b=b, n=n: e.tensor_tensor(
                out=xres[:, k * 1024 + b * 512:k * 1024 + (b + 1) * 512],
                in0=t1_sb[:], in1=x0t[n % 2][:], op=ALU.add), tag + "oev")
            E.wait("scalar", tag + "oev", E.val(tag + "oev"))
            if n >= 2:
                E.wait("scalar", tag + "ostat", n - 1)
            E.op("scalar", lambda e, k=k, b=b, n=n: e.activation(
                out=sq_sb[n % 2][:],
                in_=xres[:, k * 1024 + b * 512:k * 1024 + (b + 1) * 512],
                func=AF.Square), tag + "osq")
            def emit_stat(k=k, b=b, n=n):
                E.wait("tensor", tag + "osq", n + 1)
                def mstat(e, k=k, b=b, n=n):
                    e.matmul(stats_sum[b], lhsT=ones_bf[:],
                             rhs=xres[:, k * 1024 + b * 512:k * 1024 + (b + 1) * 512],
                             start=(k == 0), stop=(k == NKT - 1))
                    return e.matmul(stats_sq[b], lhsT=ones_bf[:], rhs=sq_sb[n % 2][:],
                                    start=(k == 0), stop=(k == NKT - 1))
                E.op("tensor", mstat, tag + "ostat")
            stat_pend.append(emit_stat)
    stat_pend.pop()()
    for b in range(2):
        E.wait("vector", tag + "ostat", E.val(tag + "ostat"))
        E.sop("vector", lambda e, b=b: e.tensor_scalar(out=SL(0), in0=stats_sum[b],
              scalar1=1.0 / HID, scalar2=None, op0=ALU.mult))
        E.sop("vector", lambda e, b=b: e.tensor_scalar(out=SL(1), in0=stats_sq[b],
              scalar1=1.0 / HID, scalar2=None, op0=ALU.mult))
        E.sop("vector", lambda e: e.tensor_tensor(out=SL(3), in0=SL(0), in1=SL(0),
              op=ALU.mult))
        E.sop("vector", lambda e: e.tensor_tensor(out=SL(2), in0=SL(1), in1=SL(3),
              op=ALU.subtract))
        E.sop("vector", lambda e: e.tensor_scalar(out=SL(3), in0=SL(2),
              scalar1=EPS, scalar2=None, op0=ALU.add), tag + "lf1")
        E.wait("scalar", tag + "lf1", E.val(tag + "lf1"))
        E.op("scalar", lambda e: e.activation(out=SL(2), in_=SL(3), func=AF.Sqrt),
             tag + "lf2")
        E.wait("vector", tag + "lf2", E.val(tag + "lf2"))
        E.sop("vector", lambda e: e.reciprocal(out=SL(4), in_=SL(2)))
        E.sop("vector", lambda e: e.tensor_tensor(out=SL(3), in0=SL(0), in1=SL(4),
              op=ALU.mult))
        E.sop("vector", lambda e: e.tensor_scalar(out=SL(5), in0=SL(3),
              scalar1=-1.0, scalar2=None, op0=ALU.mult), tag + "lf3")
        E.wait("tensor", tag + "lf3", E.val(tag + "lf3"))
        E.op("tensor", lambda e: e.matmul(pn[0][:, 0:512], lhsT=o1[:], rhs=SL(4),
             start=True, stop=True))
        E.op("tensor", lambda e: e.matmul(pn[1][:, 0:512], lhsT=o1[:], rhs=SL(5),
             start=True, stop=True), tag + "lf4")
        E.wait("vector", tag + "lf4", E.val(tag + "lf4"))
        E.sop("vector", lambda e: e.tensor_copy(out=ab_sb[:], in_=pn[0][:, 0:512]))
        E.sop("vector", lambda e: e.tensor_copy(out=bb_sb[:], in_=pn[1][:, 0:512]), tag + "lf5")
        for k in range(NKT):
            ob2 = k % 2
            n2 = b * NKT + k
            if n2 >= 2:
                E.wait("vector", tag + f"xst{ob2}", 16 * (n2 // 2))
            E.sop("vector", lambda e, k=k, b=b: e.tensor_tensor(
                out=t1_sb[:], in0=xres[:, k * 1024 + b * 512:k * 1024 + (b + 1) * 512],
                in1=ab_sb[:], op=ALU.mult))
            E.sop("vector", lambda e, ob2=ob2: e.tensor_tensor(
                out=out_sc[ob2][:], in0=t1_sb[:], in1=bb_sb[:], op=ALU.add),
                tag + "lnv")
            E.wait("gpsimd", tag + "lnv", E.val(tag + "lnv"))
            E.dma("gpsimd", x1_out[k * 128:(k + 1) * 128, b * 512:(b + 1) * 512],
                  out_sc[ob2][:], tag + f"xst{ob2}")
        if b == 0:
            E.wait("tensor", tag + "lf5", E.val(tag + "lf5"))
    for b2 in range(2):
        E.wait("gpsimd", tag + f"xst{b2}", E.val(tag + f"xst{b2}"))
        E.wait("sync", tag + f"xst{b2}", E.val(tag + f"xst{b2}"))
    return p


def prep_att_b(d, r):
    wo = d["L0_attn_Wo"].astype(np.float32)[:, PI]
    wo_swz = np.ascontiguousarray(
        wo.reshape(32, 128, 32, 128).transpose(2, 1, 0, 3).reshape(32, 128, 4096))
    bo = d["L0_attn_bo"].astype(np.float32)[PI]
    nmask = np.zeros((4, 2, 128, 512), BF16NP)
    for j in range(4):
        qpos = (8 * j + r) * 128 + np.arange(128)
        for i in range(2):
            kpos = (2 * j + i) * 512 + np.arange(512)
            nmask[j, i] = np.where(kpos[None, :] > qpos[:, None], -1e30, 0.0).astype(BF16NP)
    emask = np.zeros((8, 128, 128), BF16NP)
    for i in range(8):
        if i < r:
            emask[i] = 1.0
        elif i == r:
            emask[i] = np.tril(np.ones((128, 128), np.float32)).T.astype(BF16NP)
    return dict(
        abwo=wo_swz, abbo=np.ascontiguousarray(bo.reshape(32, 128).T),
        abnmask=np.ascontiguousarray(nmask.transpose(2, 0, 1, 3).reshape(128, 4096)),
        abemask=np.ascontiguousarray(emask.transpose(1, 0, 2).reshape(128, 1024)),
        abident=np.eye(128, dtype=np.float32),
        abo1=np.ones((1, 128), np.float32),
        abonb_in=np.ones((128, 1), BF16NP),
    )


_PROGS = {}

def _get_prog(name, builder):
    if name not in _PROGS:
        p = Prog(name)
        E = Emitter(p)
        builder(p, E)
        blk = p.block()
        E.finalize(blk)
        p.close()
        _PROGS[name] = p
    return _PROGS[name]


def kernel(**d):
    cores = list(range(NC))
    c = np.ascontiguousarray

    p1 = _get_prog("atta", lambda p, E: build_att_a(p, E, "aa"))
    maps1 = prep_att_a(d, rope_tables())
    x0 = d["L0_attn_hidden_states"]
    in1 = []
    for r in cores:
        m = dict(maps1[r])
        m["aaxb"] = c(x0[r // 4][:, PI])
        m["aaxmy"] = c(gather_tokens(x0, r)[:, PI])
        in1.append(m)
    res1 = run_bass_kernel_spmd(p1.nc, in1, core_ids=cores).results

    kt = np.zeros((2, 1024, 4096), BF16NP)
    vt = np.zeros((2, 1024, 4096), BF16NP)
    for r in cores:
        b, j4 = r // 4, r % 4
        kv = res1[r]["aakvo"]
        for jj in range(4):
            gcol = j4 * 512 + jj * 128
            if gcol < 1024:
                kt[b, gcol:gcol + 128] = kv[jj]
            else:
                vt[b, gcol - 1024:gcol - 1024 + 128] = kv[jj]

    vt_sw = c(vt.reshape(2, 8, 128, 32, 128).transpose(0, 1, 4, 3, 2)
              .reshape(2, 8, 128, 4096))

    p2 = _get_prog("l0dq", lambda p, E: build_l0_dequant(p, E, 2, 6))
    in2, meta0 = prep_l0_dq(d)
    res2 = run_bass_kernel_spmd(p2.nc, in2, core_ids=cores).results
    Wgu, Wdn = assemble_l0_dq(res2, meta0)
    wg0, wu0, wd0 = l0_streams(Wgu, Wdn)

    p3 = _get_prog("l1dq", lambda p, E: build_l1_dequant(p, E, 11, 4))
    in3, meta1 = prep_l1_dq(d)
    res3 = run_bass_kernel_spmd(p3.nc, in3, core_ids=cores).results
    Wg1, Wu1, Wd1 = assemble_l1_dq(res3, meta1)

    p4 = _get_prog("attb", lambda p, E: build_att_b(p, E, "ab"))
    in4 = []
    for r in cores:
        m = prep_att_b(d, r)
        m["abqt"] = res1[r]["aaqto"]
        m["abkt"] = kt
        m["abvt"] = vt_sw
        m["abxbfmy"] = res1[r]["aaxbfmy"]
        in4.append(m)
    res4 = run_bass_kernel_spmd(p4.nc, in4, core_ids=cores).results

    def b5(p, E):
        xin = p.din("xin", [4096, 1024], BF16)
        xout = p.dout("xout", [4096, 1024], BF16)
        wg = p.din("wg", [86, 128, 4096], BF16)
        wu = p.din("wu", [86, 128, 4096], BF16)
        wd = p.din("wd", [32, 128, 86 * 128], BF16)
        build_ffn(p, E, xin, xout, wg, wu, wd, 86, BF16, "f0")
    p5 = _get_prog("ffn0", b5)
    in5 = [dict(xin=res4[r]["abx1"], wg=wg0, wu=wu0, wd=wd0,
                f0onesb_in=np.ones((128, 1), BF16NP),
                f0ones1_in=np.ones((1, 128), np.float32)) for r in cores]
    res5 = run_bass_kernel_spmd(p5.nc, in5, core_ids=cores).results

    def b6(p, E):
        xin = p.din("xin", [4096, 1024], BF16)
        xout = p.dout("xout", [4096, 1024], BF16)
        build_l1_attn(p, E, xin, xout, BF16, "ha")
    p6 = _get_prog("hydra", b6)
    cmap = prep_l1_attn(d)
    in6 = []
    for r in cores:
        m = dict(cmap)
        m["xin"] = res5[r]["xout"]
        in6.append(m)
    res6 = run_bass_kernel_spmd(p6.nc, in6, core_ids=cores).results

    def b7(p, E):
        xin = p.din("xin", [4096, 1024], BF16)
        xout = p.dout("xout", [4096, 1024], F32)
        wg = p.din("wg", [NIT, 128, NKT * 128], BF16)
        wu = p.din("wu", [NIT, 128, NKT * 128], BF16)
        wd = p.din("wd", [NKT, 128, NIT * 128], BF16)
        build_ffn(p, E, xin, xout, wg, wu, wd, NIT, F32, "f1")
    p7 = _get_prog("ffn1", b7)
    in7 = [dict(xin=res6[r]["xout"], wg=Wg1, wu=Wu1, wd=Wd1,
                f1onesb_in=np.ones((128, 1), BF16NP),
                f1ones1_in=np.ones((1, 128), np.float32)) for r in cores]
    res7 = run_bass_kernel_spmd(p7.nc, in7, core_ids=cores).results

    out = np.empty((2, 4096, 4096), np.float32)
    for r in cores:
        x4r = res7[r]["xout"][PI_INV]
        for idx, (b, s0, s1) in enumerate(tok_slices(r)):
            out[b, s0:s1, :] = x4r[:, idx * 128:(idx + 1) * 128].T
    return out
